# version 11
# baseline (speedup 1.0000x reference)
import math
from contextlib import ExitStack

import numpy as np

import concourse.bass as bass
import concourse.mybir as mybir
from concourse.bass_utils import run_bass_kernel_spmd

F32 = mybir.dt.float32
BF16 = mybir.dt.bfloat16
I32 = mybir.dt.int32
AF = mybir.ActivationFunctionType
AL = mybir.AluOpType
AX = mybir.AxisListType

D = 1024
NFC = 8
FF = 4096
HID1 = 3072
S1 = 1.0 / math.sqrt(4.0 * math.sqrt(2.0))
KLN = float(D) * D * 1e-5
Q127 = 1.0 / 127.0

ENGS = ("tensor", "scalar", "vector", "sync", "gpsimd")


class Sem:
    def __init__(self, st, nc, name):
        self.h = st.enter_context(nc.semaphore(name))
        self.n = 0


class Ph:

    def __init__(self):
        self.ops = {e: [] for e in ENGS}

    def add(self, eng, fn):
        self.ops[eng].append(fn)

    def op(self, eng, meth, *a, inc=None, inck=1, lowp=False, **kw):
        if inc is not None:
            inc.n += inck
        tgt = inc.h if inc is not None else None

        def f(e, meth=meth, a=a, kw=kw, tgt=tgt, inck=inck, lowp=lowp):
            if lowp:
                with e.bass.allow_low_precision("bf16 output by design"):
                    ins = getattr(e, meth)(*a, **kw)
            else:
                ins = getattr(e, meth)(*a, **kw)
            if tgt is not None:
                ins.then_inc(tgt, inck)
        self.ops[eng].append(f)
        return inc.n if inc is not None else None

    def dma(self, eng, out_ap, in_ap, sem):
        sem.n += 16

        def f(e, o=out_ap, i=in_ap, h=sem.h):
            e.dma_start(o, i).then_inc(h, 16)
        self.ops[eng].append(f)
        return sem.n

    def dmat(self, out_ap, in_ap, sem):
        sem.n += 16

        def f(e, o=out_ap, i=in_ap, h=sem.h):
            e.dma_start_transpose(o, i).then_inc(h, 16)
        self.ops["sync"].append(f)
        return sem.n

    def wait(self, eng, sem, val=None):
        v = sem.n if val is None else val
        if v <= 0:
            return

        def f(e, h=sem.h, v=v):
            e.wait_ge(h, v)
        self.ops[eng].append(f)

    def emit(self, nc):
        with nc.Block() as blk:
            for e in ENGS:
                if self.ops[e]:
                    getattr(blk, e)(lambda eg, ops=tuple(self.ops[e]): [f(eg) for f in ops])



def _tile_km(a):
    k, m = a.shape
    return np.ascontiguousarray(
        a.reshape(k // 128, 128, m).transpose(1, 0, 2).reshape(128, (k // 128) * m))


def _pair_major(a):
    t = a.reshape(8, 128, 8, 128).transpose(2, 1, 0, 3)
    return np.ascontiguousarray(t.reshape(8, 128, 1024))


def _col_chunk(a, ncols):
    k, m = a.shape
    t = a.reshape(k // 128, 128, m // ncols, ncols).transpose(2, 0, 1, 3)
    return np.ascontiguousarray(t)


def host_prepare(inputs):
    f32 = np.float32
    Wqkv = np.asarray(inputs["L0_attn_Wqkv"], f32)
    bqkv = np.asarray(inputs["L0_attn_bqkv"], f32)
    idx = np.arange(16)[:, None] * 192 + np.arange(64)[None, :]
    qidx, kidx, vidx = idx.ravel(), (idx + 64).ravel(), (idx + 128).ravel()
    H = {}
    H["wq0"] = _pair_major(Wqkv[qidx].T)
    H["wk0"] = _pair_major(Wqkv[kidx].T)
    H["wv0"] = _pair_major(Wqkv[vidx].T)
    H["bq0"] = np.ascontiguousarray(bqkv[qidx].reshape(8, 128).T)
    H["bk0"] = np.ascontiguousarray(bqkv[kidx].reshape(8, 128).T)
    H["bv0"] = np.ascontiguousarray(bqkv[vidx][None, :])
    H["wo0"] = _tile_km(np.asarray(inputs["L0_attn_Wo"], f32).T)
    H["bo0"] = np.ascontiguousarray(np.asarray(inputs["L0_attn_bo"], f32).reshape(8, 128).T)
    H["w1"] = np.ascontiguousarray(np.asarray(inputs["L0_ffn_W1"], f32).T.reshape(NFC, 128, FF))
    H["b1"] = np.ascontiguousarray(np.asarray(inputs["L0_ffn_b1"], f32).reshape(FF // 128, 128).T)
    H["w2"] = np.ascontiguousarray(np.asarray(inputs["L0_ffn_W2"], f32).T.reshape(FF // 128, 128, D))
    H["b2"] = np.ascontiguousarray(np.asarray(inputs["L0_ffn_b2"], f32).reshape(8, 128).T)
    H["wq1"] = _tile_km(np.asarray(inputs["L1_attn_Wq"], f32))
    H["wk1"] = _tile_km(np.asarray(inputs["L1_attn_Wk"], f32))
    H["wv1"] = _tile_km(np.asarray(inputs["L1_attn_Wv"], f32))
    H["wo1"] = _tile_km(np.asarray(inputs["L1_attn_Wo"], f32))
    H["wg1"] = _col_chunk(np.asarray(inputs["L1_ffn_w_gate"], np.int32).T, 512)
    H["wu1"] = _col_chunk(np.asarray(inputs["L1_ffn_w_up"], np.int32).T, 512)
    H["wd1"] = _col_chunk(np.asarray(inputs["L1_ffn_w_down"], np.int32).T, 128)
    return H


def host_x_core(x_c, T):
    xt = np.ascontiguousarray(np.asarray(x_c, np.float32).T)
    return np.ascontiguousarray(xt.reshape(8, 128, T).transpose(1, 0, 2).reshape(128, 8 * T))


def host_out_core(o, T):
    return np.ascontiguousarray(o.reshape(128, 8, T).transpose(1, 0, 2).reshape(1024, T).T)



def build_program(T=2048, upto="p11", taps=()):
    assert T % 512 == 0
    NT = T // 512
    NTT = T // 128
    C1 = 256
    NCH = T // C1
    TCH = max(512, T // 2)
    NTCH = T // TCH
    NCOL = TCH // 512
    NG = NTT // 2

    PH = ["p0", "p1", "p2", "p2a", "p3", "p4", "p5", "p6", "p7", "p8", "p9", "p10", "p11"]
    stop_i = PH.index(upto)
    act = lambda name: PH.index(name) <= stop_i

    nc = bass.Bass("TRN2", target_bir_lowering=False)
    st = ExitStack()

    dram = {"x": nc.dram_tensor("x", [128, NFC * T], F32, kind="ExternalInput").ap()}
    for nm, shp, dt in [
        ("wq0", [8, 128, 1024], F32), ("wk0", [8, 128, 1024], F32), ("wv0", [8, 128, 1024], F32),
        ("bq0", [128, 8], F32), ("bk0", [128, 8], F32), ("bv0", [1, 1024], F32),
        ("wo0", [128, 8 * 1024], F32), ("bo0", [128, 8], F32),
        ("w1", [NFC, 128, FF], F32), ("b1", [128, FF // 128], F32),
        ("w2", [FF // 128, 128, D], F32), ("b2", [128, 8], F32),
        ("wq1", [128, 8 * 256], F32), ("wk1", [128, 8 * 256], F32),
        ("wv1", [128, 8 * 1024], F32), ("wo1", [128, 8 * 1024], F32),
        ("wg1", [6, 8, 128, 512], I32), ("wu1", [6, 8, 128, 512], I32),
        ("wd1", [8, 24, 128, 128], I32),
    ]:
        dram[nm] = nc.dram_tensor(nm, shp, dt, kind="ExternalInput").ap()
    dram["out"] = nc.dram_tensor("out", [128, NFC * T], F32, kind="ExternalOutput").ap()
    dram["w1b"] = nc.dram_tensor("w1b", [NFC, 128, FF], BF16, kind="Internal").ap()
    dram["w2b"] = nc.dram_tensor("w2b", [FF // 128, 128, D], BF16, kind="Internal").ap()

    tap_outs = {}
    for tp in taps:
        shp = {"xln": [128, NFC * T], "y0": [128, NFC * T], "x1": [128, NFC * T],
               "x2": [128, NFC * T], "x3": [128, NFC * T], "y1": [128, NFC * T],
               "qf1": [128, 2 * T], "kf1": [128, 2 * T],
               "qtok": [128, NTT * 256], "ktok": [128, NTT * 256],
               "vtok": [128, NTT * 1040]}[tp]
        dt = F32 if tp in ("x1", "x2", "x3") else BF16
        tap_outs[tp] = nc.dram_tensor("tap_" + tp, shp, dt, kind="ExternalOutput").ap()

    def tap(name, tensor):
        if name not in tap_outs:
            return
        p = Ph()
        p.dma("sync", tap_outs[name], tensor[:, :], dout)
        p.wait("sync", dout)
        p.emit(nc)

    dc = Sem(st, nc, "dc")
    dwp = [Sem(st, nc, f"dwp{i}") for i in range(2)]
    dsl = [Sem(st, nc, f"dsl{i}") for i in range(2)]
    dst = [Sem(st, nc, f"dst{i}") for i in range(2)]
    dw1 = [Sem(st, nc, f"dw1{i}") for i in range(2)]
    dw2 = [Sem(st, nc, f"dw2{i}") for i in range(2)]
    dq1 = Sem(st, nc, "dq1")
    dtr = [Sem(st, nc, f"dtr{i}") for i in range(3)]
    dgu = [Sem(st, nc, f"dgu{i}") for i in range(2)]
    dwd = [Sem(st, nc, f"dwd{i}") for i in range(2)]
    dout = Sem(st, nc, "dout")
    ts = Sem(st, nc, "ts")
    ss = Sem(st, nc, "ss")
    vs = Sem(st, nc, "vs")

    sbuf = lambda name, shp, dt, stk=st: stk.enter_context(nc.sbuf_tensor(name, shp, dt))

    X = sbuf("X", [128, NFC * T], F32)
    ONES_B = sbuf("ONES_B", [128, 128], BF16)
    ONES_F = sbuf("ONES_F", [128, 128], F32)
    NRM2 = sbuf("NRM2", [128, 2], BF16)
    TRIL = sbuf("TRIL", [128, 128], BF16)
    IOTA = sbuf("IOTA", [128, 128], F32)
    BQ = sbuf("BQ", [128, 8], F32)
    BQ8 = sbuf("BQ8", [128, 8], F32)
    BK = sbuf("BK", [128, 8], F32)
    BVR = sbuf("BVR", [1, 1024], BF16)
    BO = sbuf("BO", [128, 8], F32)
    B1 = sbuf("B1", [128, FF // 128], F32)
    B2 = sbuf("B2", [128, 8], F32)
    CB = sbuf("CB", [128, 3], F32)
    HMASK = sbuf("HMASK", [128, 8], F32)
    IOTA16 = sbuf("IOTA16", [128, 8], F32)
    IDN = sbuf("IDN", [128, 128], BF16)

    _ln_ctr = [0]

    def ln_phase(XLN):
        _ln_ctr[0] += 1
        sfx = str(_ln_ctr[0])
        with ExitStack() as ps:
            XB = ps.enter_context(nc.sbuf_tensor("lnXB" + sfx, [128, NFC * 512], BF16))
            XSQ = ps.enter_context(nc.sbuf_tensor("lnXSQ" + sfx, [128, NFC * 512], BF16))
            TMPA = ps.enter_context(nc.sbuf_tensor("lnTMP" + sfx, [128, NFC * 512], BF16))
            ROWS = ps.enter_context(nc.sbuf_tensor("lnROWS" + sfx, [1, 7 * 512], F32))
            RBF = ps.enter_context(nc.sbuf_tensor("lnRBF" + sfx, [1, 2 * 512], BF16))
            RB = ps.enter_context(nc.sbuf_tensor("lnRB" + sfx, [128, 512], BF16))
            MRB = ps.enter_context(nc.sbuf_tensor("lnMRB" + sfx, [128, 512], BF16))
            psA = ps.enter_context(nc.psum_tensor("lnA" + sfx, [1, 512], F32))
            psB = ps.enter_context(nc.psum_tensor("lnB" + sfx, [1, 512], F32))
            psR = ps.enter_context(nc.psum_tensor("lnR" + sfx, [128, 512], F32))
            psM = ps.enter_context(nc.psum_tensor("lnM" + sfx, [128, 512], F32))
            sS, s2S = ROWS[:, 0:512], ROWS[:, 512:1024]
            uS, wS = ROWS[:, 1024:1536], ROWS[:, 1536:2048]
            lnwS, rS, mrS = ROWS[:, 2048:2560], ROWS[:, 2560:3072], ROWS[:, 3072:3584]

            p = Ph()
            prev_apply_vs = 0
            prev_stat_read_vs = 0
            prev_bcast_evac_ss = 0
            for tc in range(NT):
                p.wait("scalar", vs, prev_apply_vs)
                for c in range(NFC):
                    p.op("scalar", "activation", XB[:, c * 512:(c + 1) * 512],
                         X[:, c * T + tc * 512: c * T + tc * 512 + 512], AF.Copy,
                         inc=(ss if c == NFC - 1 else None))
                cast_ss = ss.n
                p.wait("vector", ss, cast_ss)
                for c in range(NFC):
                    p.op("vector", "tensor_mul", XSQ[:, c * 512:(c + 1) * 512],
                         XB[:, c * 512:(c + 1) * 512], XB[:, c * 512:(c + 1) * 512],
                         inc=(vs if c == NFC - 1 else None))
                sq_vs = vs.n
                p.wait("tensor", ss, cast_ss)
                p.wait("tensor", vs, prev_stat_read_vs)
                for c in range(NFC):
                    p.op("tensor", "matmul", psA[0:1, :], ONES_B[:, 0:1],
                         XB[:, c * 512:(c + 1) * 512], start=(c == 0), stop=(c == NFC - 1),
                         inc=(ts if c == NFC - 1 else None))
                tsA = ts.n
                p.wait("tensor", vs, sq_vs)
                for c in range(NFC):
                    p.op("tensor", "matmul", psB[0:1, :], ONES_B[:, 0:1],
                         XSQ[:, c * 512:(c + 1) * 512], start=(c == 0), stop=(c == NFC - 1),
                         inc=(ts if c == NFC - 1 else None))
                tsB = ts.n
                p.wait("vector", ts, tsA)
                p.op("vector", "tensor_copy", sS, psA[0:1, :])
                p.wait("vector", ts, tsB)
                p.op("vector", "tensor_copy", s2S, psB[0:1, :])
                p.op("vector", "drain")
                p.op("vector", "tensor_mul", uS, sS, sS)
                p.op("vector", "drain")
                p.op("vector", "scalar_tensor_tensor", wS, s2S, 1024.0, uS,
                     AL.mult, AL.subtract, inc=vs)
                prev_stat_read_vs = vs.n
                p.wait("scalar", vs, prev_stat_read_vs)
                p.op("scalar", "activation", lnwS, wS, AF.Ln, scale=2.0 ** -20,
                     bias=CB[0:1, 0:1])
                p.op("scalar", "drain")
                p.op("scalar", "activation", rS, lnwS, AF.Exp, scale=-0.5, inc=ss)
                p.wait("vector", ss)
                p.op("vector", "tensor_copy", RBF[:, 0:512], rS)
                p.op("vector", "scalar_tensor_tensor", mrS, sS, 1.0 / 1024.0, rS,
                     AL.mult, AL.mult)
                p.op("vector", "drain")
                p.op("vector", "tensor_copy", RBF[:, 512:1024], mrS, inc=vs)
                p.wait("tensor", vs)
                p.wait("tensor", ss, prev_bcast_evac_ss)
                p.op("tensor", "matmul", psR[:, :], ONES_B[0:1, :], RBF[:, 0:512],
                     start=True, stop=True)
                p.op("tensor", "matmul", psM[:, :], ONES_B[0:1, :], RBF[:, 512:1024],
                     start=True, stop=True, inc=ts)
                p.wait("scalar", ts)
                p.wait("scalar", vs, prev_apply_vs)
                p.op("scalar", "activation", RB[:, :], psR[:, :], AF.Copy)
                p.op("scalar", "activation", MRB[:, :], psM[:, :], AF.Copy, inc=ss)
                prev_bcast_evac_ss = ss.n
                p.wait("vector", ss)
                for c in range(NFC):
                    p.op("vector", "tensor_mul", TMPA[:, c * 512:(c + 1) * 512],
                         XB[:, c * 512:(c + 1) * 512], RB[:, :])
                p.op("vector", "drain")
                for c in range(NFC):
                    p.op("vector", "tensor_sub",
                         XLN[:, c * T + tc * 512: c * T + tc * 512 + 512],
                         TMPA[:, c * 512:(c + 1) * 512], MRB[:, :],
                         inc=(vs if c == NFC - 1 else None))
                prev_apply_vs = vs.n
            p.emit(nc)

    es0 = ExitStack()
    XLN = sbuf("XLN0", [128, NFC * T], BF16, es0)
    es0a = ExitStack()
    Y = sbuf("Y0", [128, NFC * T], BF16, es0a)
    WQP = [sbuf(f"WQP{i}", [128, 1024], BF16, es0a) for i in range(2)]
    WKP = [sbuf(f"WKP{i}", [128, 1024], BF16, es0a) for i in range(2)]
    WVP = [sbuf(f"WVP{i}", [128, 1024], BF16, es0a) for i in range(2)]

    p = Ph()
    p.dma("gpsimd", X[:, :], dram["x"], dc)
    p.dma("gpsimd", WQP[0][:, :], dram["wq0"][0], dc)
    p.dma("gpsimd", WKP[0][:, :], dram["wk0"][0], dc)
    p.dma("gpsimd", WVP[0][:, :], dram["wv0"][0], dc)
    p.dma("gpsimd", BQ[:, :], dram["bq0"], dc)
    p.dma("gpsimd", BK[:, :], dram["bk0"], dc)
    p.dma("gpsimd", BVR[:, :], dram["bv0"], dc)
    p.dma("gpsimd", BO[:, :], dram["bo0"], dc)
    p.dma("gpsimd", B1[:, :], dram["b1"], dc)
    p.dma("gpsimd", B2[:, :], dram["b2"], dc)
    wpair_done = [(dc, dc.n)]
    p.add("gpsimd", lambda g: g.iota(IOTA[:, :], [[1, 128]], channel_multiplier=-1,
                                     allow_small_or_imprecise_dtypes=True))
    p.add("gpsimd", lambda g: g.iota(IOTA16[:, :], [[16, 8]], channel_multiplier=-1,
                                     allow_small_or_imprecise_dtypes=True))
    p.add("vector", lambda v: v.memset(ONES_B[:, :], 1.0))
    p.add("vector", lambda v: v.memset(ONES_F[:, :], 1.0))
    p.add("vector", lambda v: v.memset(NRM2[:, :], 0.0))
    p.add("vector", lambda v: v.drain())
    p.add("vector", lambda v: v.memset(NRM2[0:64, 0:1], 1.0))
    p.add("vector", lambda v: v.memset(NRM2[64:128, 1:2], 1.0))
    p.add("vector", lambda v: v.memset(CB[:, 0:1], 1e-5))
    p.add("vector", lambda v: v.memset(CB[:, 2:3], 0.5))
    p.emit(nc)

    p = Ph()
    p.add("vector", lambda v: v.tensor_single_scalar(TRIL[:, :], IOTA[:, :], 0.0, AL.is_ge))
    p.add("vector", lambda v: v.tensor_single_scalar(IDN[:, :], IOTA[:, :], 0.0, AL.is_equal))
    p.add("vector", lambda v: v.tensor_single_scalar(HMASK[:, :], IOTA16[:, :], 0.0, AL.is_le))
    p.add("vector", lambda v: v.tensor_single_scalar(IOTA16[:, :], IOTA16[:, :], -16.0, AL.is_gt))
    p.add("vector", lambda v: v.drain())
    p.add("vector", lambda v: v.tensor_mul(HMASK[:, :], HMASK[:, :], IOTA16[:, :]))
    p.wait("vector", dc)
    p.add("vector", lambda v: v.tensor_scalar_mul(BQ8[:, :], BQ[:, :], 0.125))
    p.emit(nc)

    if act("p0"):
        ln_phase(XLN)
    tap("xln", XLN)

    if act("p1"):
        with ExitStack() as ps:
            QP = [ps.enter_context(nc.sbuf_tensor(f"QP{i}", [128, T], BF16)) for i in range(2)]
            KP = [ps.enter_context(nc.sbuf_tensor(f"KP{i}", [128, T], BF16)) for i in range(2)]
            VP = [ps.enter_context(nc.sbuf_tensor(f"VP{i}", [128, NTT * 130], BF16)) for i in range(2)]
            SCR = ps.enter_context(nc.sbuf_tensor("SCR", [128, 2048 + 2 * T], BF16))
            PSB = [SCR[:, 0:1024], SCR[:, 1024:2048]]
            QS, KS = SCR[:, 2048:2048 + T], SCR[:, 2048 + T:2048 + 2 * T]
            NRED = ps.enter_context(nc.sbuf_tensor("NRED", [1, 4 * NT], F32))
            MXR = ps.enter_context(nc.sbuf_tensor("MXR", [1, 8], F32))
            NEGC1 = ps.enter_context(nc.sbuf_tensor("NEGC1", [1, 1], F32))
            NEGC = [ps.enter_context(nc.sbuf_tensor(f"NEGCB{i}", [128, 1], F32)) for i in range(2)]
            DROW = ps.enter_context(nc.sbuf_tensor("DROW", [65, NT * 1024], BF16))
            WSL = [ps.enter_context(nc.sbuf_tensor(f"WSL{i}", [128, 1024], F32)) for i in range(2)]
            WTB = [ps.enter_context(nc.sbuf_tensor(f"WTB{i}", [128, 1024], BF16)) for i in range(2)]
            TMP1 = ps.enter_context(nc.sbuf_tensor("terTMP", [128, 1024], F32))
            RACC = ps.enter_context(nc.sbuf_tensor("terRACC", [128, 64], F32))
            RS1 = ps.enter_context(nc.sbuf_tensor("terRS1", [128, 1], F32))
            THRR = ps.enter_context(nc.sbuf_tensor("terTHRR", [1, 2], F32))
            THRV = ps.enter_context(nc.sbuf_tensor("terTHRV", [128, 2], F32))
            S4 = ps.enter_context(nc.psum_tensor("S4", [128, 2048], F32))
            PO = ps.enter_context(nc.psum_tensor("PO", [65, 1024], F32))
            PAUXT = ps.enter_context(nc.psum_tensor("PAUX", [128, 1024], F32))
            PAX = [PAUXT[:, 0:512], PAUXT[:, 512:1024]]

            NKT = T // 128
            p = Ph()
            for i in range(2):
                p.add("vector", lambda v, i=i: v.memset(
                    VP[i].rearrange("p (t f) -> p t f", t=NTT)[:, :, 64:65], 1.0))
                p.add("vector", lambda v, i=i: v.memset(
                    VP[i].rearrange("p (t f) -> p t f", t=NTT)[:, :, 129:130], 1.0))
            p.op("vector", "drain", inc=vs)

            trk = {"pax_free": [vs.n, vs.n],
                   "pax_i": 0,
                   "wread_ts": [0, 0],
                   "prep_done": [vs.n, vs.n],
                   "attn_ts": [0] * 10,
                   "last_exp_ss": 0,
                   "o_evac_vs": 0,
                   "zmul_vs": 0}
            w12b_done = []

            def prep_chunks(pr):
                pb = pr % 2
                if pr + 1 < 8:
                    p.wait("gpsimd", ts, trk["wread_ts"][1 - pb])
                    p.dma("gpsimd", WQP[1 - pb][:, :], dram["wq0"][pr + 1], dwp[1 - pb])
                    p.dma("gpsimd", WKP[1 - pb][:, :], dram["wk0"][pr + 1], dwp[1 - pb])
                    p.dma("gpsimd", WVP[1 - pb][:, :], dram["wv0"][pr + 1], dwp[1 - pb])
                    wpair_done.append((dwp[1 - pb], dwp[1 - pb].n))
                wsem, wval = wpair_done[pr]
                p.wait("tensor", wsem, wval)
                if pr >= 2:
                    p.wait("vector", ts, trk["attn_ts"][pr - 2])
                for which in range(2):
                    WB = WQP[pb] if which == 0 else WKP[pb]
                    DSTT = QP[pb] if which == 0 else KP[pb]
                    for tc in range(NT):
                        px = trk["pax_i"] = 1 - trk["pax_i"]
                        p.wait("tensor", vs, trk["pax_free"][px])
                        for c in range(NFC):
                            p.op("tensor", "matmul", PAX[px][:, :], WB[:, c * 128:(c + 1) * 128],
                                 XLN[:, c * T + tc * 512: c * T + tc * 512 + 512],
                                 start=(c == 0), stop=(c == NFC - 1),
                                 inc=(ts if c == NFC - 1 else None))
                        p.wait("vector", ts)
                        if which == 0:
                            p.op("vector", "tensor_scalar", DSTT[:, tc * 512:(tc + 1) * 512],
                                 PAX[px][:, :], 0.125, BQ8[:, pr:pr + 1], AL.mult, AL.add,
                                 inc=vs)
                        else:
                            p.op("vector", "tensor_copy", DSTT[:, tc * 512:(tc + 1) * 512],
                                 PAX[px][:, :], inc=vs)
                        trk["pax_free"][px] = vs.n
                        yield pr
                for tt in range(NTT):
                    px = trk["pax_i"] = 1 - trk["pax_i"]
                    p.wait("tensor", vs, trk["pax_free"][px])
                    p.op("tensor", "matmul", PAX[px][:, 0:128], ONES_B[0:1, 0:128],
                         BVR[:, pr * 128:(pr + 1) * 128],
                         start=True, stop=False)
                    for c in range(NFC):
                        p.op("tensor", "matmul", PAX[px][:, 0:128],
                             XLN[:, c * T + tt * 128: c * T + tt * 128 + 128],
                             WVP[pb][:, c * 128:(c + 1) * 128],
                             start=False, stop=(c == NFC - 1),
                             inc=(ts if c == NFC - 1 else None))
                    p.wait("vector", ts)
                    p.op("vector", "tensor_copy",
                         VP[pb].rearrange("p (t b f) -> p t b f", t=NTT, b=2)[:, tt, :, 0:64],
                         PAX[px][:, 0:128].rearrange("p (b f) -> p b f", b=2), inc=vs)
                    trk["pax_free"][px] = vs.n
                    if tt % 2 == 1:
                        yield pr
                trk["wread_ts"][pb] = ts.n
                p.wait("vector", vs, trk["prep_done"][pb])
                p.op("vector", "tensor_mul", QS, QP[pb][:, :], QP[pb][:, :])
                p.op("vector", "tensor_mul", KS, KP[pb][:, :], KP[pb][:, :], inc=vs)
                sq_vs = vs.n
                yield pr
                p.wait("tensor", vs, sq_vs)
                for which, SRC in enumerate([QS, KS]):
                    for aa in range(2):
                        for tc in range(NT):
                            g = (which * 2 + aa) * NT + tc
                            px = trk["pax_i"] = 1 - trk["pax_i"]
                            p.wait("tensor", vs, trk["pax_free"][px])
                            p.op("tensor", "matmul", PAX[px][0:1, 0:512], NRM2[:, aa:aa + 1],
                                 SRC[:, tc * 512:(tc + 1) * 512], start=True, stop=True,
                                 inc=ts)
                            p.wait("vector", ts)
                            p.op("vector", "reduce_max", NRED[0:1, g:g + 1], PAX[px][0:1, 0:512],
                                 axis=AX.X, inc=vs)
                            trk["pax_free"][px] = vs.n
                        yield pr
                p.op("vector", "drain")
                for j in range(4):
                    p.op("vector", "reduce_max", MXR[0:1, j:j + 1],
                         NRED[0:1, j * NT:(j + 1) * NT], axis=AX.X)
                p.op("vector", "drain")
                p.op("vector", "tensor_mul", MXR[0:1, 4:5], MXR[0:1, 0:1], MXR[0:1, 2:3])
                p.op("vector", "tensor_mul", MXR[0:1, 5:6], MXR[0:1, 1:2], MXR[0:1, 3:4])
                p.op("vector", "drain")
                p.op("vector", "reduce_max", MXR[0:1, 6:7], MXR[0:1, 4:6], axis=AX.X)
                p.op("vector", "drain", inc=vs)
                yield pr
                p.wait("scalar", vs)
                p.op("scalar", "activation", MXR[0:1, 7:8], MXR[0:1, 6:7], AF.Ln)
                p.op("scalar", "drain")
                p.op("scalar", "activation", NEGC1[:, :], MXR[0:1, 7:8], AF.Exp,
                     scale=0.5, inc=ss)
                p.wait("vector", ss)
                p.op("vector", "tensor_scalar_mul", NEGC1[:, :], NEGC1[:, :], -1.0)
                p.op("vector", "drain", inc=vs)
                yield pr
                p.wait("tensor", vs)
                p.wait("tensor", vs, trk["pax_free"][0])
                p.op("tensor", "matmul", PAX[0][:, 1:2], ONES_F[0:1, 0:128], NEGC1[:, :],
                     start=True, stop=True, inc=ts)
                p.wait("vector", ts)
                p.op("vector", "tensor_copy", NEGC[pb][:, :], PAX[0][:, 1:2], inc=vs)
                trk["pax_free"][0] = vs.n
                trk["prep_done"][pb] = vs.n
                yield pr

            def ternary_chunks():
                w1q = dram["w1"].rearrange("c p (q f) -> c q p f", f=1024)
                w1bq = dram["w1b"].rearrange("c p (q f) -> c q p f", f=1024)
                for wdram, wbdram in [(lambda i: w1q[i // 4, i % 4], lambda i: w1bq[i // 4, i % 4]),
                                      (lambda i: dram["w2"][i], lambda i: dram["w2b"][i])]:
                    nslab = 32
                    red_v0 = vs.n
                    p.wait("gpsimd", vs)
                    p.dma("gpsimd", WSL[0][:, :], wdram(0), dsl[0])
                    p.dma("gpsimd", WSL[1][:, :], wdram(1), dsl[1])
                    for s_ in range(nslab):
                        yield 99
                        yield 99
                        p.wait("vector", dsl[s_ % 2])
                        p.op("vector", "reduce_sum", RACC[:, s_:s_ + 1], WSL[s_ % 2][:, :],
                             axis=AX.X, apply_absolute_value=True, inc=vs)
                        if s_ + 2 < nslab:
                            p.wait("gpsimd", vs)
                            p.dma("gpsimd", WSL[s_ % 2][:, :], wdram(s_ + 2), dsl[s_ % 2])
                        yield 99
                    p.op("vector", "drain")
                    p.op("vector", "reduce_sum", RS1[:, 0:1], RACC[:, 0:nslab], axis=AX.X,
                         inc=vs)
                    yield 99
                    p.wait("tensor", vs)
                    p.wait("tensor", vs, trk["pax_free"][0])
                    p.op("tensor", "matmul", PAX[0][0:1, 2:3], ONES_F[:, 0:1], RS1[:, 0:1],
                         start=True, stop=True, inc=ts)
                    p.wait("vector", ts)
                    p.op("vector", "tensor_scalar", THRR[:, 0:1], PAX[0][0:1, 2:3],
                         1.0 / (nslab * 128.0 * 1024.0), 1e-8, AL.mult, AL.max)
                    p.op("vector", "drain")
                    p.op("vector", "tensor_scalar_mul", THRR[:, 0:1], THRR[:, 0:1], 0.5)
                    p.op("vector", "drain")
                    p.op("vector", "tensor_scalar_mul", THRR[:, 1:2], THRR[:, 0:1], -1.0)
                    p.op("vector", "drain", inc=vs)
                    yield 99
                    p.wait("tensor", vs)
                    p.op("tensor", "matmul", PAX[0][:, 3:5], ONES_F[0:1, 0:128], THRR[:, 0:2],
                         start=True, stop=True, inc=ts)
                    p.wait("vector", ts)
                    p.op("vector", "tensor_copy", THRV[:, :], PAX[0][:, 3:5])
                    p.op("vector", "drain", inc=vs)
                    trk["pax_free"][0] = vs.n
                    yield 99
                    ter_v0 = vs.n
                    p.wait("vector", dst[0])
                    p.wait("vector", dst[1])
                    p.wait("gpsimd", vs)
                    p.dma("gpsimd", WSL[0][:, :], wdram(0), dsl[0])
                    p.dma("gpsimd", WSL[1][:, :], wdram(1), dsl[1])
                    for s_ in range(nslab):
                        yield 99
                        yield 99
                        p.wait("vector", dsl[s_ % 2])
                        if s_ >= 2:
                            p.wait("vector", dst[s_ % 2])
                        p.op("vector", "tensor_single_scalar", TMP1[:, :], WSL[s_ % 2][:, :],
                             THRV[:, 1:2], AL.is_lt)
                        p.op("vector", "drain")
                        p.op("vector", "scalar_tensor_tensor", WTB[s_ % 2][:, :],
                             WSL[s_ % 2][:, :], THRV[:, 0:1], TMP1[:, :],
                             AL.is_gt, AL.subtract)
                        p.op("vector", "drain", inc=vs)
                        p.wait("sync", vs)
                        p.dma("sync", wbdram(s_), WTB[s_ % 2][:, :], dst[s_ % 2])
                        if s_ + 2 < nslab:
                            p.wait("gpsimd", vs)
                            p.dma("gpsimd", WSL[s_ % 2][:, :], wdram(s_ + 2), dsl[s_ % 2])
                        yield 99
                w12b_done.append((dst[0], dst[0].n))
                w12b_done.append((dst[1], dst[1].n))
                yield 99

            tern = ternary_chunks()
            tern_done = [False]
            cur_prep = [None]

            def pump(n=1):
                for _ in range(n):
                    if cur_prep[0] is not None:
                        try:
                            next(cur_prep[0])
                            continue
                        except StopIteration:
                            cur_prep[0] = None
                    if not tern_done[0]:
                        try:
                            next(tern)
                        except StopIteration:
                            tern_done[0] = True

            for _ in prep_chunks(0):
                pass

            for pr in range(8):
                pb = pr % 2
                cur_prep[0] = prep_chunks(pr + 1) if pr + 1 < 8 else None
                p.wait("tensor", vs, trk["prep_done"][pb])
                p.wait("tensor", ss, trk["last_exp_ss"])
                for qc in range(NT):
                    pv_ts = {}
                    exp_ss = {}
                    for kt in range(NKT + 1):
                        if kt < NKT:
                            prev = exp_ss.get(kt - 2, trk["last_exp_ss"])
                            p.wait("tensor", ss, prev)
                            sb = (kt % 2) * 1024
                            for aa in range(2):
                                p.op("tensor", "matmul",
                                     S4[:, sb + aa * 512: sb + aa * 512 + 512],
                                     KP[pb][64 * aa:64 * aa + 64, kt * 128:(kt + 1) * 128],
                                     QP[pb][64 * aa:64 * aa + 64, qc * 512:(qc + 1) * 512],
                                     start=True, stop=True,
                                     inc=(ts if aa == 1 else None))
                            s2_ts = ts.n
                        if kt > 0:
                            pk = kt - 1
                            p.wait("tensor", ss, exp_ss[pk])
                            if pk == 0:
                                p.wait("tensor", vs, trk["o_evac_vs"])
                                p.wait("tensor", vs, trk["zmul_vs"])
                            for aa in range(2):
                                p.op("tensor", "matmul", PO[:, aa * 512:(aa + 1) * 512],
                                     VP[pb][:, pk * 130 + aa * 65: pk * 130 + aa * 65 + 65],
                                     PSB[pk % 2][:, aa * 512:(aa + 1) * 512],
                                     start=(pk == 0), stop=(pk == NKT - 1),
                                     skip_group_check=True,
                                     inc=(ts if aa == 1 else None))
                            pv_ts[pk] = ts.n
                        if kt < NKT:
                            p.wait("scalar", ts, pv_ts.get(kt - 2, 0))
                            p.wait("scalar", ts, s2_ts)
                            p.op("scalar", "activation", PSB[kt % 2][:, :],
                                 S4[:, (kt % 2) * 1024:(kt % 2) * 1024 + 1024], AF.Exp,
                                 bias=NEGC[pb][:, 0:1], inc=ss)
                            exp_ss[(kt)] = ss.n
                        pump()
                    trk["last_exp_ss"] = exp_ss[NKT - 1]
                    o_ts = ts.n
                    p.wait("vector", ts, o_ts)
                    for aa in range(2):
                        p.op("vector", "tensor_copy",
                             Y[64 * aa:64 * aa + 64, pr * T + qc * 512: pr * T + qc * 512 + 512],
                             PO[0:64, aa * 512: aa * 512 + 512])
                        p.op("vector", "tensor_copy",
                             DROW[64:65, (qc * 2 + aa) * 512: (qc * 2 + aa) * 512 + 512],
                             PO[64:65, aa * 512: aa * 512 + 512],
                             inc=(vs if aa == 1 else None))
                    trk["o_evac_vs"] = vs.n
                trk["attn_ts"][pr] = ts.n
                p.op("vector", "drain")
                for g in range(NT * 2):
                    p.op("vector", "reciprocal", DROW[64:65, g * 512:(g + 1) * 512],
                         DROW[64:65, g * 512:(g + 1) * 512], lowp=True,
                         inc=(vs if g == NT * 2 - 1 else None))
                zr_vs = vs.n
                for qc in range(NT):
                    for aa in range(2):
                        g = qc * 2 + aa
                        p.wait("tensor", vs, zr_vs)
                        p.wait("tensor", vs, trk["zmul_vs"])
                        p.op("tensor", "matmul", PO[0:64, 0:512], ONES_B[64:65, 0:64],
                             DROW[64:65, g * 512:(g + 1) * 512], start=True, stop=True,
                             skip_group_check=True, inc=ts)
                        zb_t = ts.n
                        p.wait("vector", ts, zb_t)
                        yslc = Y[64 * aa:64 * aa + 64, pr * T + qc * 512: pr * T + qc * 512 + 512]
                        p.op("vector", "tensor_mul", yslc, yslc, PO[0:64, 0:512], inc=vs)
                        trk["zmul_vs"] = vs.n
                        pump()
                while cur_prep[0] is not None:
                    pump()
            while cur_prep[0] is not None or not tern_done[0]:
                pump()
            p.emit(nc)
    tap("y0", Y)

    if act("p2"):
        with ExitStack() as ps:
            WO0 = ps.enter_context(nc.sbuf_tensor("WO0", [128, 8 * 1024], BF16))
            PWO = ps.enter_context(nc.psum_tensor("PWO", [128, 1024], F32))
            T1 = ps.enter_context(nc.sbuf_tensor("T1", [128, 1024], F32))
            p = Ph()
            p.dma("gpsimd", WO0[:, :], dram["wo0"], dc)
            p.wait("tensor", dc)
            wo_ev0 = ss.n
            add_v0 = vs.n
            for g in range(NFC * NT):
                fc, tc = divmod(g, NT)
                if g >= 2:
                    p.wait("tensor", ss, wo_ev0 + g - 1)
                pw = PWO[:, (g % 2) * 512:(g % 2) * 512 + 512]
                for c in range(NFC):
                    p.op("tensor", "matmul", pw,
                         WO0[:, c * 1024 + fc * 128: c * 1024 + fc * 128 + 128],
                         Y[:, c * T + tc * 512: c * T + tc * 512 + 512],
                         start=(c == 0), stop=(c == NFC - 1),
                         inc=(ts if c == NFC - 1 else None))
                p.wait("scalar", ts)
                if g >= 2:
                    p.wait("scalar", vs, add_v0 + g - 1)
                p.op("scalar", "activation", T1[:, (g % 2) * 512:(g % 2) * 512 + 512], pw,
                     AF.Identity, bias=BO[:, fc:fc + 1], inc=ss)
                p.wait("vector", ss)
                p.op("vector", "tensor_add",
                     X[:, fc * T + tc * 512: fc * T + tc * 512 + 512],
                     X[:, fc * T + tc * 512: fc * T + tc * 512 + 512],
                     T1[:, (g % 2) * 512:(g % 2) * 512 + 512], inc=vs)
            p.emit(nc)
    es0a.close()
    tap("x1", X)

    if act("p3"):
        ln_phase(XLN)

    if act("p4"):
        with ExitStack() as ps:
            HSB = ps.enter_context(nc.sbuf_tensor("HSB", [128, (FF // 128) * TCH], BF16))
            W1S = [ps.enter_context(nc.sbuf_tensor(f"W1S{i}", [128, NFC * 512], BF16)) for i in range(2)]
            W2S = [ps.enter_context(nc.sbuf_tensor(f"W2S{i}", [128, (FF // 128) * 128], BF16)) for i in range(2)]
            TF1 = ps.enter_context(nc.sbuf_tensor("TF1", [128, 1024], F32))
            PH1 = ps.enter_context(nc.psum_tensor("PH1", [128, 2048], F32))
            w1v = dram["w1b"].rearrange("c p (b f) -> b c p f", f=512)
            w2v = dram["w2b"].rearrange("k p (b f) -> b k p f", f=128)
            NFB = FF // 512
            p = Ph()
            w1s_done, w2s_done = [], []
            for sm, v in w12b_done:
                p.wait("sync", sm, v)
            for i in range(2):
                p.dma("sync", W1S[i].rearrange("p (c f) -> p c f", f=512), w1v[i].rearrange("c p f -> p c f"), dw1[i])
                w1s_done.append((dw1[i], dw1[i].n))
                p.dma("sync", W2S[i].rearrange("p (k f) -> p k f", f=128), w2v[i].rearrange("k p f -> p k f"), dw2[i])
                w2s_done.append((dw2[i], dw2[i].n))
            h_mm_ts = {}
            for tch in range(NTCH):
                p.wait("tensor", ss)
                gel_s0 = ss.n
                for fb in range(NFB):
                    gi = tch * NFB + fb
                    p.wait("tensor", *w1s_done[gi])
                    for fi in range(4):
                        for col in range(NCOL):
                            g = (fb * 4 + fi) * NCOL + col
                            if g >= 4:
                                p.wait("tensor", ss, gel_s0 + g - 3)
                            pq = PH1[:, (g % 4) * 512:(g % 4) * 512 + 512]
                            for c in range(NFC):
                                p.op("tensor", "matmul", pq,
                                     W1S[fb % 2][:, c * 512 + fi * 128: c * 512 + fi * 128 + 128],
                                     XLN[:, c * T + tch * TCH + col * 512: c * T + tch * TCH + col * 512 + 512],
                                     start=(c == 0), stop=(c == NFC - 1),
                                     inc=(ts if c == NFC - 1 else None))
                            p.wait("scalar", ts)
                            fidx = fb * 4 + fi
                            p.op("scalar", "activation",
                                 HSB[:, fidx * TCH + col * 512: fidx * TCH + col * 512 + 512],
                                 pq, AF.Gelu, bias=B1[:, fidx:fidx + 1], inc=ss)
                    h_mm_ts[gi] = ts.n
                    nxt = gi + 2
                    if nxt < NTCH * NFB:
                        p.wait("sync", ts, h_mm_ts[gi])
                        p.dma("sync", W1S[nxt % 2].rearrange("p (c f) -> p c f", f=512),
                              w1v[nxt % NFB].rearrange("c p f -> p c f"), dw1[nxt % 2])
                        w1s_done.append((dw1[nxt % 2], dw1[nxt % 2].n))
                h_done = ss.n
                p.wait("scalar", vs)
                add_v0 = vs.n
                dn_ev0 = ss.n
                for dcc in range(NFC):
                    gi = tch * NFC + dcc
                    p.wait("tensor", *w2s_done[gi])
                    p.wait("tensor", ss, h_done)
                    for col in range(NCOL):
                        g = dcc * NCOL + col
                        if g >= 2:
                            p.wait("tensor", ss, dn_ev0 + g - 1)
                        pq = PH1[:, (g % 2) * 512:(g % 2) * 512 + 512]
                        for k in range(FF // 128):
                            p.op("tensor", "matmul", pq,
                                 W2S[dcc % 2][:, k * 128:(k + 1) * 128],
                                 HSB[:, k * TCH + col * 512: k * TCH + col * 512 + 512],
                                 start=(k == 0), stop=(k == FF // 128 - 1),
                                 inc=(ts if k == FF // 128 - 1 else None))
                        p.wait("scalar", ts)
                        if g >= 2:
                            p.wait("scalar", vs, add_v0 + g - 1)
                        p.op("scalar", "activation", TF1[:, (g % 2) * 512:(g % 2) * 512 + 512],
                             pq, AF.Identity, bias=B2[:, dcc:dcc + 1], inc=ss)
                        p.wait("vector", ss)
                        p.op("vector", "tensor_add",
                             X[:, dcc * T + tch * TCH + col * 512: dcc * T + tch * TCH + col * 512 + 512],
                             X[:, dcc * T + tch * TCH + col * 512: dcc * T + tch * TCH + col * 512 + 512],
                             TF1[:, (g % 2) * 512:(g % 2) * 512 + 512], inc=vs)
                    dn_mm = ts.n
                    nxt = gi + 2
                    if nxt < NTCH * NFC:
                        p.wait("sync", ts, dn_mm)
                        p.dma("sync", W2S[nxt % 2].rearrange("p (k f) -> p k f", f=128),
                              w2v[nxt % NFC].rearrange("k p f -> p k f"), dw2[nxt % 2])
                        w2s_done.append((dw2[nxt % 2], dw2[nxt % 2].n))
            p.emit(nc)
    es0.close()
    tap("x2", X)

    if act("p5"):
        es1 = ExitStack()
        QF = sbuf("QF", [128, 2 * T], BF16, es1)
        KF = sbuf("KF", [128, 2 * T], BF16, es1)
        QTK = sbuf("QTK", [128, NTT * 256], BF16, es1)
        KTK = sbuf("KTK", [128, NTT * 256], BF16, es1)
        V1 = sbuf("V1", [128, NTT * 1040], BF16, es1)
        es1x = ExitStack()
        XLN = sbuf("XLN1", [128, NFC * T], BF16, es1x)

        ln_phase(XLN)

    if act("p6"):
        with ExitStack() as ps:
            WQ1 = ps.enter_context(nc.sbuf_tensor("WQ1", [128, 8 * 256], BF16))
            WK1 = ps.enter_context(nc.sbuf_tensor("WK1", [128, 8 * 256], BF16))
            WV1 = ps.enter_context(nc.sbuf_tensor("WV1", [128, 8 * 1024], BF16))
            PP1 = ps.enter_context(nc.psum_tensor("PP1", [128, 1024], F32))
            PV1 = ps.enter_context(nc.psum_tensor("PV1", [128, 1024], F32))
            p = Ph()
            p.dma("gpsimd", WQ1[:, :], dram["wq1"], dq1)
            p.dma("gpsimd", WK1[:, :], dram["wk1"], dq1)
            p.dma("gpsimd", WV1[:, :], dram["wv1"], dq1)
            wqkv1 = dq1.n
            p.add("vector", lambda v: v.memset(
                V1.rearrange("p (t h f) -> p t h f", t=NTT, h=16)[:, :, :, 64:65], 1.0))
            p.op("vector", "drain", inc=vs)
            v1_ones_vs = vs.n
            p.wait("tensor", dq1, wqkv1)
            ev0 = ss.n
            for which, (WB, DST) in enumerate([(WQ1, QF), (WK1, KF)]):
                for fc in range(2):
                    for tc in range(NT):
                        g = which * 2 * NT + fc * NT + tc
                        if g >= 2:
                            p.wait("tensor", ss, ev0 + g - 1)
                        pq = PP1[:, (g % 2) * 512:(g % 2) * 512 + 512]
                        for c in range(NFC):
                            p.op("tensor", "matmul", pq,
                                 WB[:, c * 256 + fc * 128: c * 256 + fc * 128 + 128],
                                 XLN[:, c * T + tc * 512: c * T + tc * 512 + 512],
                                 start=(c == 0), stop=(c == NFC - 1),
                                 inc=(ts if c == NFC - 1 else None))
                        p.wait("scalar", ts)
                        p.op("scalar", "activation",
                             DST[:, fc * T + tc * 512: fc * T + tc * 512 + 512], pq,
                             AF.Copy, inc=ss)
            p.wait("tensor", ss)
            ev1 = ss.n
            for which, (WB, DST) in enumerate([(WQ1, QTK), (WK1, KTK)]):
                kscl = S1 if which == 0 else S1 / math.sqrt(2.0)
                for tt in range(NTT):
                    g = which * NTT + tt
                    if g >= 2:
                        p.wait("tensor", ss, ev1 + g - 1)
                    pq = PP1[:, (g % 2) * 512: (g % 2) * 512 + 256]
                    for c in range(NFC):
                        p.op("tensor", "matmul", pq,
                             XLN[:, c * T + tt * 128: c * T + tt * 128 + 128],
                             WB[:, c * 256:(c + 1) * 256],
                             start=(c == 0), stop=(c == NFC - 1),
                             inc=(ts if c == NFC - 1 else None))
                    p.wait("scalar", ts)
                    p.op("scalar", "activation", DST[:, tt * 256:(tt + 1) * 256], pq,
                         AF.Identity, scale=kscl, inc=ss)
            p.wait("tensor", vs, v1_ones_vs)
            ev2 = vs.n
            for tt in range(NTT):
                for half in range(2):
                    g = tt * 2 + half
                    if g >= 2:
                        p.wait("tensor", vs, ev2 + g - 1)
                    pq = PV1[:, (g % 2) * 512:(g % 2) * 512 + 512]
                    for c in range(NFC):
                        p.op("tensor", "matmul", pq,
                             XLN[:, c * T + tt * 128: c * T + tt * 128 + 128],
                             WV1[:, c * 1024 + half * 512: c * 1024 + half * 512 + 512],
                             start=(c == 0), stop=(c == NFC - 1),
                             inc=(ts if c == NFC - 1 else None))
                    p.wait("vector", ts)
                    p.op("vector", "tensor_copy",
                         V1.rearrange("p (t h f) -> p t h f", t=NTT, h=16)[:, tt, half * 8:(half + 1) * 8, 0:64],
                         pq.rearrange("p (h f) -> p h f", h=8), inc=vs)
            p.emit(nc)
        tap("qf1", QF)
        tap("kf1", KF)
        tap("qtok", QTK)
        tap("ktok", KTK)
        tap("vtok", V1)
    if act("p5"):
        es1x.close()

    if act("p7"):
        es1y = ExitStack()
        Y = sbuf("Y1", [128, NFC * T], BF16, es1y)
        WO1 = sbuf("WO1b", [128, 8 * 1024], BF16, es1y)
        with ExitStack() as ps:
            KFB = [[ps.enter_context(nc.sbuf_tensor(f"KFB{i}_{j}", [128, 2 * 384], BF16))
                    for j in range(2)] for i in range(2)]
            QFTK = [ps.enter_context(nc.sbuf_tensor(f"QFTK{i}", [128, 2 * 384], BF16)) for i in range(2)]
            QFM = [ps.enter_context(nc.sbuf_tensor(f"QFM{i}", [128, 3 * 256], BF16)) for i in range(3)]
            RHQ = [ps.enter_context(nc.sbuf_tensor(f"RHQ{i}", [128, 256], BF16)) for i in range(2)]
            ARAW = [ps.enter_context(nc.sbuf_tensor(f"ARAW{i}", [128, 384], BF16)) for i in range(2)]
            ASB = [ps.enter_context(nc.sbuf_tensor(f"ASB{i}", [128, 384], BF16)) for i in range(2)]
            STSB = [ps.enter_context(nc.sbuf_tensor(f"STSB{i}", [128, 195], BF16)) for i in range(2)]
            Z1 = [ps.enter_context(nc.sbuf_tensor(f"Z1_{i}", [128, 256], BF16)) for i in range(2)]
            ZSC1 = [ps.enter_context(nc.sbuf_tensor(f"ZSC1_{i}", [65, 256], F32)) for i in range(2)]
            YT1 = [ps.enter_context(nc.sbuf_tensor(f"YT1_{i}", [64, 256], BF16)) for i in range(2)]
            PST = [ps.enter_context(nc.psum_tensor(f"PST{i}", [128, 195], F32)) for i in range(2)]
            PG = [ps.enter_context(nc.psum_tensor(f"PG{i}", [128, 384], F32)) for i in range(2)]
            PY = [ps.enter_context(nc.psum_tensor(f"PY{i}", [65, 256], F32)) for i in range(2)]
            PTR = ps.enter_context(nc.psum_tensor("PTR", [128, 768], BF16))
            PZB = ps.enter_context(nc.psum_tensor("PZB7", [64, 256], F32))

            p = Ph()
            p.dma("gpsimd", WO1[:, :], dram["wo1"], dq1)
            for bb in KFB:
                for b in bb:
                    p.add("vector", lambda v, b=b: v.memset(
                        b.rearrange("p (s f) -> p s f", s=2)[:, :, 0:1], 0.5))
                    p.add("vector", lambda v, b=b: v.memset(
                        b.rearrange("p (s f) -> p s f", s=2)[:, :, 273:384], 0.0))
            for b in QFTK:
                p.add("vector", lambda v, b=b: v.memset(
                    b.rearrange("p (s f) -> p s f", s=2)[:, :, 0:1], 1.0))
                p.add("vector", lambda v, b=b: v.memset(
                    b.rearrange("p (s f) -> p s f", s=2)[:, :, 273:384], 0.0))
            p.op("vector", "drain", inc=vs)

            units = [(hp + 8 * lane, ch) for hp in range(8) for ch in range(NCH)
                     for lane in (0, 1)]

            def build_qft(h, ch, buf, wait_tr):
                if wait_tr:
                    p.wait("vector", ts, wait_tr)
                for tsub in range(2):
                    tt = 2 * ch + tsub
                    src = QTK[:, tt * 256 + h * 16: tt * 256 + h * 16 + 16]
                    i0 = src.unsqueeze(1).broadcast_to([128, 16, 16])
                    i1 = src.unsqueeze(2).broadcast_to([128, 16, 16])
                    p.op("vector", "tensor_tensor",
                         QFTK[buf][:, tsub * 384 + 17: tsub * 384 + 273].rearrange(
                             "p (i j) -> p i j", i=16), i0, i1, AL.mult)
                    p.op("vector", "tensor_scalar_mul",
                         QFTK[buf][:, tsub * 384 + 1: tsub * 384 + 17], src, 0.5 / S1,
                         inc=(vs if tsub == 1 else None))
                return vs.n

            ptr_evac_vs = [0, 0]
            tr_ts = {}

            def issue_qft_T(buf, wait_vs, wait_ts):
                tr_end = 0
                for tsub in range(2):
                    p.wait("tensor", vs, wait_vs)
                    p.wait("tensor", vs, ptr_evac_vs[tsub])
                    for fcx in range(3):
                        p.op("tensor", "transpose", PTR[tsub][:, fcx * 128:(fcx + 1) * 128],
                             QFTK[buf][:, tsub * 384 + fcx * 128: tsub * 384 + fcx * 128 + 128],
                             IDN[:, :], inc=(ts if fcx == 2 else None))
                    p.wait("vector", ts)
                    p.wait("vector", ts, wait_ts)
                    p.op("vector", "tensor_copy",
                         QFM[buf].rearrange("p (c f) -> p c f", f=128)[:, tsub:6:2, :],
                         PTR[tsub].rearrange("p (c f) -> p c f", f=128), inc=vs)
                    ptr_evac_vs[tsub] = vs.n
                    tr_end = ts.n
                return tr_end

            g_ts = {}; araw_ss = {}; a_vs = {}; y_ts = {}
            yt_vs = {}; yv_vs = {}; state_ts = {}
            evac_ss = {}
            qfm_done = {}

            def prep_qf(tu):
                nh, nch = units[tu]
                if nch > 0:
                    bv = build_qft(nh, nch, tu % 2, qfm_done.get(tu - 2) or 0)
                    qfm_done[tu] = issue_qft_T(tu % 2, bv, y_ts.get(tu - 2, 0))
                else:
                    qfm_done[tu] = None

            qfm_done[0] = None
            if len(units) > 1:
                prep_qf(1)

            def U(i):
                return units[i]

            def v_build(i):
                h, ch = U(i)
                lane = h // 8
                kb = (i // 2) % 2
                p.wait("vector", ts, g_ts.get(i - 2, 0))
                p.op("vector", "tensor_single_scalar", RHQ[lane][:, :],
                     QF[:, lane * T + ch * 256: lane * T + ch * 256 + 256],
                     HMASK[:, (h % 8):(h % 8) + 1], AL.mult)
                p.wait("vector", ts, state_ts.get(i - 4, 0))
                src3 = KTK.rearrange("p (t r) -> p t r", r=256)[:, 2 * ch:2 * ch + 2,
                                                                h * 16:h * 16 + 16]
                i0 = src3.unsqueeze(2).broadcast_to([128, 2, 16, 16])
                i1 = src3.unsqueeze(3).broadcast_to([128, 2, 16, 16])
                kview = KFB[lane][kb].rearrange("p (s r) -> p s r", s=2)
                p.op("vector", "tensor_tensor",
                     kview[:, :, 17:273].rearrange("p s (i j) -> p s i j", i=16),
                     i0, i1, AL.mult)
                p.op("vector", "tensor_scalar_mul", kview[:, :, 1:17], src3,
                     0.25 * math.sqrt(2.0) / S1)
                p.op("vector", "drain", inc=vs)
                build_vs[i] = vs.n

            def v_qft(i):
                h, ch = U(i)
                if ch == 0:
                    return
                buf = i % 2
                p.wait("vector", ts, tr_ts.get(i - 2, 0))
                src3 = QTK.rearrange("p (t r) -> p t r", r=256)[:, 2 * ch:2 * ch + 2,
                                                                h * 16:h * 16 + 16]
                i0 = src3.unsqueeze(2).broadcast_to([128, 2, 16, 16])
                i1 = src3.unsqueeze(3).broadcast_to([128, 2, 16, 16])
                qview = QFTK[buf].rearrange("p (s r) -> p s r", s=2)
                p.op("vector", "tensor_tensor",
                     qview[:, :, 17:273].rearrange("p s (i j) -> p s i j", i=16),
                     i0, i1, AL.mult)
                p.op("vector", "tensor_scalar_mul", qview[:, :, 1:17], src3, 0.5 / S1,
                     inc=vs)
                qft_vs[i] = vs.n

            def t_tr(i):
                h, ch = U(i)
                if ch == 0:
                    tr_ts[i] = ts.n
                    return
                p.wait("tensor", vs, qft_vs[i])
                p.wait("tensor", ss, ptr_evac_vs.get(i - 2, 0))
                for tsub in range(2):
                    for fcx in range(3):
                        p.op("tensor", "transpose",
                             PTR[:, tsub * 384 + fcx * 128: tsub * 384 + fcx * 128 + 128],
                             QFTK[i % 2][:, tsub * 384 + fcx * 128: tsub * 384 + fcx * 128 + 128],
                             IDN[:, :],
                             inc=(ts if (tsub, fcx) == (1, 2) else None))
                tr_ts[i] = ts.n

            def v_qfm_evac(i):
                h, ch = U(i)
                if ch == 0:
                    return
                p.wait("scalar", ts, tr_ts[i])
                p.wait("scalar", ts, y_ts.get(i - 3, 0))
                src = PTR.rearrange("p (t c f) -> p t c f", t=2, c=3)
                dst = QFM[i % 3].rearrange("p (c t f) -> p t c f", c=3, t=2)
                p.op("scalar", "activation", dst, src, AF.Copy, inc=ss)
                ptr_evac_vs[i] = ss.n

            def s_evac(i):
                h, ch = U(i)
                lane = h // 8
                if ch == 0:
                    return
                p.wait("scalar", ts, state_ts.get(i - 2, 0))
                p.wait("scalar", ts, y_ts.get(i - 2, 0))
                p.op("scalar", "activation", STSB[lane][:, :], PST[lane][:, :],
                     AF.Copy, inc=ss)
                evac_ss[i] = ss.n

            def t_g_state(i):
                h, ch = U(i)
                lane = h // 8
                kb = (i // 2) % 2
                p.wait("tensor", vs, build_vs[i])
                if ch > 0:
                    p.wait("tensor", ss, evac_ss[i])
                p.wait("tensor", ss, araw_ss.get(i - 2, 0))
                p.op("tensor", "matmul", PG[lane][:, 0:256],
                     KF[:, lane * T + (2 * ch) * 128: lane * T + (2 * ch) * 128 + 128],
                     RHQ[lane][:, :], start=True, stop=True)
                p.op("tensor", "matmul", PG[lane][:, 256:384],
                     KF[:, lane * T + (2 * ch + 1) * 128: lane * T + (2 * ch + 1) * 128 + 128],
                     RHQ[lane][:, 128:256], start=True, stop=True, inc=ts)
                g_ts[i] = ts.n
                for tsub in range(2):
                    first = (ch == 0 and tsub == 0)
                    for fcx in range(3):
                        p.op("tensor", "matmul",
                             PST[lane][:, fcx * 65: fcx * 65 + 65],
                             KFB[lane][kb][:, tsub * 384 + fcx * 128: tsub * 384 + fcx * 128 + 128],
                             V1[:, (2 * ch + tsub) * 1040 + h * 65: (2 * ch + tsub) * 1040 + h * 65 + 65],
                             start=(first and fcx == 0), stop=(tsub == 1 and fcx == 2),
                             skip_group_check=True,
                             inc=(ts if (tsub == 1 and fcx == 2) else None))
                state_ts[i] = ts.n

            def s_square(i):
                h, ch = U(i)
                lane = h // 8
                p.wait("scalar", ts, g_ts[i])
                p.wait("scalar", vs, a_vs.get(i - 2, 0))
                p.op("scalar", "activation", ARAW[lane][:, 0:384], PG[lane][:, 0:384],
                     AF.Square, scale=0.125, bias=CB[:, 2:3], inc=ss)
                araw_ss[i] = ss.n

            def v_poly(i):
                h, ch = U(i)
                lane = h // 8
                p.wait("vector", ss, araw_ss[i])
                p.wait("vector", ts, y_ts.get(i - 2, 0))
                p.op("vector", "scalar_tensor_tensor", ASB[lane][:, 0:128],
                     ARAW[lane][:, 0:128], 0.25, TRIL[:, :], AL.add, AL.mult)
                p.op("vector", "tensor_scalar_add", ASB[lane][:, 128:256],
                     ARAW[lane][:, 128:256], 0.25)
                p.op("vector", "scalar_tensor_tensor", ASB[lane][:, 256:384],
                     ARAW[lane][:, 256:384], 0.25, TRIL[:, :], AL.add, AL.mult, inc=vs)
                a_vs[i] = vs.n

            def t_y(i):
                h, ch = U(i)
                lane = h // 8
                p.wait("tensor", vs, a_vs[i])
                p.wait("tensor", vs, yv_vs.get(i - 2, 0))
                mms = [
                    (PY[lane][:, 0:256],
                     V1[:, (2 * ch) * 1040 + h * 65: (2 * ch) * 1040 + h * 65 + 65],
                     ASB[lane][:, 0:256]),
                    (PY[lane][:, 128:256],
                     V1[:, (2 * ch + 1) * 1040 + h * 65: (2 * ch + 1) * 1040 + h * 65 + 65],
                     ASB[lane][:, 256:384]),
                ]
                if ch > 0:
                    mms += [
                        (PY[lane][:, 0:256], STSB[lane][:, 0:65], QFM[i % 3][:, 0:256]),
                        (PY[lane][:, 0:256], STSB[lane][:, 65:130], QFM[i % 3][:, 256:512]),
                        (PY[lane][:, 0:256], STSB[lane][0:17, 130:195], QFM[i % 3][0:17, 512:768]),
                    ]
                for mi, (o, l, r) in enumerate(mms):
                    p.op("tensor", "matmul", o, l, r, start=(mi == 0),
                         stop=(mi == len(mms) - 1),
                         inc=(ts if mi == len(mms) - 1 else None))
                y_ts[i] = ts.n

            def s_z(i):
                h, ch = U(i)
                lane = h // 8
                p.wait("scalar", ts, y_ts[i])
                p.wait("scalar", ts, zb_ts.get(i - 2, 0))
                p.op("scalar", "activation", ZSC1[lane][64:65, :], PY[lane][64:65, :], AF.Ln)
                p.op("scalar", "drain")
                p.op("scalar", "activation", Z1[lane][64:65, :], ZSC1[lane][64:65, :],
                     AF.Exp, scale=-1.0, inc=ss)
                z_ss[i] = ss.n

            def t_zb(i):
                h, ch = U(i)
                lane = h // 8
                p.wait("tensor", ss, z_ss[i])
                p.wait("tensor", vs, yv_vs.get(i - 1, 0))
                p.op("tensor", "matmul", PZB[:, :], ONES_B[64:65, 0:64],
                     Z1[lane][64:65, :], start=True, stop=True, inc=ts)
                zb_ts[i] = ts.n

            def v_yout(i):
                h, ch = U(i)
                lane = h // 8
                p.wait("scalar", ts, y_ts[i])
                p.wait("scalar", vs, yv_vs.get(i - 2, 0))
                p.op("scalar", "activation", YT1[lane][:, :], PY[lane][0:64, :], AF.Copy,
                     inc=ss)
                yt_s = ss.n
                p.wait("vector", ss, yt_s)
                p.wait("vector", ts, zb_ts[i])
                p.op("vector", "tensor_mul",
                     Y[64 * (h % 2): 64 * (h % 2) + 64,
                       (h // 2) * T + ch * 256: (h // 2) * T + ch * 256 + 256],
                     YT1[lane][:, :], PZB[0:64, :], inc=vs)
                yv_vs[i] = vs.n

            g_ts = {}; araw_ss = {}; a_vs = {}; y_ts = {}
            yv_vs = {}; state_ts = {}; evac_ss = {}
            build_vs = {}; qft_vs = {}; tr_ts = {}; ptr_evac_vs = {}
            zb_ts = {}; z_ss = {}
            NU = len(units)

            v_build(0)
            for j in (1, 2):
                if j < NU:
                    v_qft(j)

            for i in range(NU):
                t_g_state(i)
                if i >= 1:
                    t_zb(i - 1)
                if i + 2 < NU:
                    t_tr(i + 2)
                if i + 1 < NU:
                    s_evac(i + 1)
                s_square(i)
                v_poly(i)
                if i + 2 < NU:
                    v_qfm_evac(i + 2)
                if i + 1 < NU:
                    v_build(i + 1)
                if i + 3 < NU:
                    v_qft(i + 3)
                t_y(i)
                s_z(i)
                if i >= 1:
                    v_yout(i - 1)
            t_zb(NU - 1)
            v_yout(NU - 1)
            p.emit(nc)
        tap("y1", Y)

    if act("p8"):
        with ExitStack() as ps:
            PWO = ps.enter_context(nc.psum_tensor("PWO1", [128, 1024], F32))
            T1 = ps.enter_context(nc.sbuf_tensor("T1b", [128, 1024], F32))
            p = Ph()
            p.wait("tensor", dq1)
            wo_ev0 = ss.n
            add_v0 = vs.n
            for g in range(NFC * NT):
                fc, tc = divmod(g, NT)
                if g >= 2:
                    p.wait("tensor", ss, wo_ev0 + g - 1)
                pw = PWO[:, (g % 2) * 512:(g % 2) * 512 + 512]
                for c in range(NFC):
                    p.op("tensor", "matmul", pw,
                         WO1[:, c * 1024 + fc * 128: c * 1024 + fc * 128 + 128],
                         Y[:, c * T + tc * 512: c * T + tc * 512 + 512],
                         start=(c == 0), stop=(c == NFC - 1),
                         inc=(ts if c == NFC - 1 else None))
                p.wait("scalar", ts)
                if g >= 2:
                    p.wait("scalar", vs, add_v0 + g - 1)
                p.op("scalar", "activation", T1[:, (g % 2) * 512:(g % 2) * 512 + 512], pw,
                     AF.Copy, inc=ss)
                p.wait("vector", ss)
                p.op("vector", "tensor_add",
                     X[:, fc * T + tc * 512: fc * T + tc * 512 + 512],
                     X[:, fc * T + tc * 512: fc * T + tc * 512 + 512],
                     T1[:, (g % 2) * 512:(g % 2) * 512 + 512], inc=vs)
            p.emit(nc)
    if act("p7"):
        es1y.close()
    if act("p5"):
        es1.close()
    tap("x3", X)

    if act("p9"):
        es3 = ExitStack()
        XLN = sbuf("XLN3", [128, NFC * T], BF16, es3)
        ln_phase(XLN)

    if act("p10"):
        with ExitStack() as ps:
            HH = ps.enter_context(nc.sbuf_tensor("HH", [128, (HID1 // 128) * TCH], BF16))
            WGS = [ps.enter_context(nc.sbuf_tensor(f"WGS{i}", [128, NFC * 512], BF16)) for i in range(2)]
            WUS = [ps.enter_context(nc.sbuf_tensor(f"WUS{i}", [128, NFC * 512], BF16)) for i in range(2)]
            WDS = [ps.enter_context(nc.sbuf_tensor(f"WDS{i}", [128, (HID1 // 128) * 128], BF16)) for i in range(2)]
            SGT = ps.enter_context(nc.sbuf_tensor("SGT", [128, 512], BF16))
            TT1 = ps.enter_context(nc.sbuf_tensor("TT1", [128, 512], BF16))
            TF2 = ps.enter_context(nc.sbuf_tensor("TF2", [128, 1024], F32))
            PGU = ps.enter_context(nc.psum_tensor("PGU", [128, 2048], F32))
            PDN = ps.enter_context(nc.psum_tensor("PDN", [128, 1024], F32))
            NFB1 = HID1 // 512
            p = Ph()
            wg_done, wd_done = [], []
            hh_hist = {}
            for i in range(2):
                p.dma("gpsimd", WGS[i].rearrange("p (c f) -> p c f", f=512), dram["wg1"][i].rearrange("c p f -> p c f"), dgu[i])
                p.dma("gpsimd", WUS[i].rearrange("p (c f) -> p c f", f=512), dram["wu1"][i].rearrange("c p f -> p c f"), dgu[i])
                wg_done.append((dgu[i], dgu[i].n))
                p.dma("gpsimd", WDS[i].rearrange("p (k f) -> p k f", f=128), dram["wd1"][i].rearrange("k p f -> p k f"), dwd[i])
                wd_done.append((dwd[i], dwd[i].n))
            for tch in range(NTCH):
                for fb in range(NFB1):
                    gi = tch * NFB1 + fb
                    p.wait("tensor", *wg_done[gi])
                    for fi in range(4):
                        for col in range(NCOL):
                            g = gi * 4 * NCOL + fi * NCOL + col
                            if g > 1:
                                p.wait("tensor", vs, hh_hist.get(g - 2, 0))
                            pg_ = PGU[:, (g % 2) * 512:(g % 2) * 512 + 512]
                            pu_ = PGU[:, 1024 + (g % 2) * 512: 1024 + (g % 2) * 512 + 512]
                            for c in range(NFC):
                                p.op("tensor", "matmul", pg_,
                                     WGS[fb % 2][:, c * 512 + fi * 128: c * 512 + fi * 128 + 128],
                                     XLN[:, c * T + tch * TCH + col * 512: c * T + tch * TCH + col * 512 + 512],
                                     start=(c == 0), stop=(c == NFC - 1),
                                     inc=(ts if c == NFC - 1 else None))
                            g_ts_ = ts.n
                            for c in range(NFC):
                                p.op("tensor", "matmul", pu_,
                                     WUS[fb % 2][:, c * 512 + fi * 128: c * 512 + fi * 128 + 128],
                                     XLN[:, c * T + tch * TCH + col * 512: c * T + tch * TCH + col * 512 + 512],
                                     start=(c == 0), stop=(c == NFC - 1),
                                     inc=(ts if c == NFC - 1 else None))
                            u_ts_ = ts.n
                            p.wait("scalar", ts, g_ts_)
                            p.wait("scalar", vs, hh_vs if g > 0 else 0)
                            p.op("scalar", "activation", SGT[:, :], pg_, AF.Sigmoid,
                                 scale=Q127, inc=ss)
                            p.wait("vector", ss)
                            p.op("vector", "tensor_mul", TT1[:, :], SGT[:, :], pg_)
                            p.wait("vector", ts, u_ts_)
                            p.op("vector", "drain")
                            fidx = fb * 4 + fi
                            p.op("vector", "tensor_mul",
                                 HH[:, fidx * TCH + col * 512: fidx * TCH + col * 512 + 512],
                                 TT1[:, :], pu_, inc=vs)
                            hh_vs = vs.n
                            hh_hist[g] = vs.n
                    gu_mm = ts.n
                    nxt = gi + 2
                    if nxt < NTCH * NFB1:
                        p.wait("gpsimd", ts, gu_mm)
                        p.dma("gpsimd", WGS[nxt % 2].rearrange("p (c f) -> p c f", f=512),
                              dram["wg1"][nxt % NFB1].rearrange("c p f -> p c f"), dgu[nxt % 2])
                        p.dma("gpsimd", WUS[nxt % 2].rearrange("p (c f) -> p c f", f=512),
                              dram["wu1"][nxt % NFB1].rearrange("c p f -> p c f"), dgu[nxt % 2])
                        wg_done.append((dgu[nxt % 2], dgu[nxt % 2].n))
                h1_vs = vs.n
                p.wait("tensor", ss)
                p.wait("scalar", vs)
                dn_ev0 = ss.n
                add_v0 = vs.n
                for dcc in range(NFC):
                    gi = tch * NFC + dcc
                    p.wait("tensor", *wd_done[gi])
                    p.wait("tensor", vs, h1_vs)
                    for col in range(NCOL):
                        g = dcc * NCOL + col
                        if g >= 2:
                            p.wait("tensor", ss, dn_ev0 + g - 1)
                        pq = PDN[:, (g % 2) * 512:(g % 2) * 512 + 512]
                        for k in range(HID1 // 128):
                            p.op("tensor", "matmul", pq,
                                 WDS[dcc % 2][:, k * 128:(k + 1) * 128],
                                 HH[:, k * TCH + col * 512: k * TCH + col * 512 + 512],
                                 start=(k == 0), stop=(k == HID1 // 128 - 1),
                                 inc=(ts if k == HID1 // 128 - 1 else None))
                        p.wait("scalar", ts)
                        if g >= 2:
                            p.wait("scalar", vs, add_v0 + g - 1)
                        p.op("scalar", "activation", TF2[:, (g % 2) * 512:(g % 2) * 512 + 512],
                             pq, AF.Identity, scale=Q127 ** 3, inc=ss)
                        p.wait("vector", ss)
                        p.op("vector", "tensor_add",
                             X[:, dcc * T + tch * TCH + col * 512: dcc * T + tch * TCH + col * 512 + 512],
                             X[:, dcc * T + tch * TCH + col * 512: dcc * T + tch * TCH + col * 512 + 512],
                             TF2[:, (g % 2) * 512:(g % 2) * 512 + 512], inc=vs)
                    dn_mm = ts.n
                    nxt = gi + 2
                    if nxt < NTCH * NFC:
                        p.wait("gpsimd", ts, dn_mm)
                        p.dma("gpsimd", WDS[nxt % 2].rearrange("p (k f) -> p k f", f=128),
                              dram["wd1"][nxt % NFC].rearrange("k p f -> p k f"), dwd[nxt % 2])
                        wd_done.append((dwd[nxt % 2], dwd[nxt % 2].n))
            p.emit(nc)

    p = Ph()
    p.dma("sync", dram["out"], X[:, :], dout)
    p.wait("sync", dout)
    p.emit(nc)

    if act("p9"):
        es3.close()
    st.close()
    nc.finalize()
    return nc



def kernel(**inputs):
    T = 2048
    x = np.asarray(inputs["L0_attn_x"], np.float32)
    H = host_prepare(inputs)
    nc = build_program(T=T)
    in_maps = []
    for c in range(8):
        m = dict(H)
        m["x"] = host_x_core(x[c], T)
        in_maps.append(m)
    res = run_bass_kernel_spmd(nc, in_maps, core_ids=list(range(8)))
    out = np.stack([host_out_core(np.asarray(r["out"]), T) for r in res.results])
    dt = np.float64 if out.dtype == np.float64 else np.float32
    return np.ascontiguousarray(out, dtype=dt)


# version 13
# speedup vs baseline: 1.1908x; 1.1908x over previous
import math
from contextlib import ExitStack

import numpy as np

import concourse.bass as bass
import concourse.mybir as mybir
from concourse.bass_utils import run_bass_kernel_spmd

F32 = mybir.dt.float32
BF16 = mybir.dt.bfloat16
I32 = mybir.dt.int32
AF = mybir.ActivationFunctionType
AL = mybir.AluOpType
AX = mybir.AxisListType

D = 1024
NFC = 8
FF = 4096
HID1 = 3072
S1 = 1.0 / math.sqrt(4.0 * math.sqrt(2.0))
KLN = float(D) * D * 1e-5
Q127 = 1.0 / 127.0

ENGS = ("tensor", "scalar", "vector", "sync", "gpsimd")


class Sem:
    def __init__(self, st, nc, name):
        self.h = st.enter_context(nc.semaphore(name))
        self.n = 0


class Ph:

    def __init__(self):
        self.ops = {e: [] for e in ENGS}

    def add(self, eng, fn):
        self.ops[eng].append(fn)

    def op(self, eng, meth, *a, inc=None, inck=1, lowp=False, **kw):
        if inc is not None:
            inc.n += inck
        tgt = inc.h if inc is not None else None

        def f(e, meth=meth, a=a, kw=kw, tgt=tgt, inck=inck, lowp=lowp):
            if lowp:
                with e.bass.allow_low_precision("bf16 output by design"):
                    ins = getattr(e, meth)(*a, **kw)
            else:
                ins = getattr(e, meth)(*a, **kw)
            if tgt is not None:
                ins.then_inc(tgt, inck)
        self.ops[eng].append(f)
        return inc.n if inc is not None else None

    def dma(self, eng, out_ap, in_ap, sem):
        sem.n += 16

        def f(e, o=out_ap, i=in_ap, h=sem.h):
            e.dma_start(o, i).then_inc(h, 16)
        self.ops[eng].append(f)
        return sem.n

    def dmat(self, out_ap, in_ap, sem):
        sem.n += 16

        def f(e, o=out_ap, i=in_ap, h=sem.h):
            e.dma_start_transpose(o, i).then_inc(h, 16)
        self.ops["sync"].append(f)
        return sem.n

    def wait(self, eng, sem, val=None):
        v = sem.n if val is None else val
        if v <= 0:
            return

        def f(e, h=sem.h, v=v):
            e.wait_ge(h, v)
        self.ops[eng].append(f)

    def emit(self, nc):
        with nc.Block() as blk:
            for e in ENGS:
                if self.ops[e]:
                    getattr(blk, e)(lambda eg, ops=tuple(self.ops[e]): [f(eg) for f in ops])



def _tile_km(a):
    k, m = a.shape
    return np.ascontiguousarray(
        a.reshape(k // 128, 128, m).transpose(1, 0, 2).reshape(128, (k // 128) * m))


def _pair_major(a):
    t = a.reshape(8, 128, 8, 128).transpose(2, 1, 0, 3)
    return np.ascontiguousarray(t.reshape(8, 128, 1024))


def _col_chunk(a, ncols):
    k, m = a.shape
    t = a.reshape(k // 128, 128, m // ncols, ncols).transpose(2, 0, 1, 3)
    return np.ascontiguousarray(t)


def host_prepare(inputs):
    f32 = np.float32
    Wqkv = np.asarray(inputs["L0_attn_Wqkv"], f32)
    bqkv = np.asarray(inputs["L0_attn_bqkv"], f32)
    idx = np.arange(16)[:, None] * 192 + np.arange(64)[None, :]
    qidx, kidx, vidx = idx.ravel(), (idx + 64).ravel(), (idx + 128).ravel()
    H = {}
    H["wq0"] = _pair_major(Wqkv[qidx].T)
    H["wk0"] = _pair_major(Wqkv[kidx].T)
    H["wv0"] = _pair_major(Wqkv[vidx].T)
    H["bq0"] = np.ascontiguousarray(bqkv[qidx].reshape(8, 128).T)
    H["bk0"] = np.ascontiguousarray(bqkv[kidx].reshape(8, 128).T)
    H["bv0"] = np.ascontiguousarray(bqkv[vidx][None, :])
    H["wo0"] = _tile_km(np.asarray(inputs["L0_attn_Wo"], f32).T)
    H["bo0"] = np.ascontiguousarray(np.asarray(inputs["L0_attn_bo"], f32).reshape(8, 128).T)
    H["w1"] = np.ascontiguousarray(np.asarray(inputs["L0_ffn_W1"], f32).T.reshape(NFC, 128, FF))
    H["b1"] = np.ascontiguousarray(np.asarray(inputs["L0_ffn_b1"], f32).reshape(FF // 128, 128).T)
    H["w2"] = np.ascontiguousarray(np.asarray(inputs["L0_ffn_W2"], f32).T.reshape(FF // 128, 128, D))
    H["b2"] = np.ascontiguousarray(np.asarray(inputs["L0_ffn_b2"], f32).reshape(8, 128).T)
    H["wq1"] = _tile_km(np.asarray(inputs["L1_attn_Wq"], f32))
    H["wk1"] = _tile_km(np.asarray(inputs["L1_attn_Wk"], f32))
    H["wv1"] = _tile_km(np.asarray(inputs["L1_attn_Wv"], f32))
    H["wo1"] = _tile_km(np.asarray(inputs["L1_attn_Wo"], f32))
    H["wg1"] = _col_chunk(np.asarray(inputs["L1_ffn_w_gate"], np.int32).T, 512)
    H["wu1"] = _col_chunk(np.asarray(inputs["L1_ffn_w_up"], np.int32).T, 512)
    H["wd1"] = _col_chunk(np.asarray(inputs["L1_ffn_w_down"], np.int32).T, 128)
    return H


def host_x_core(x_c, T):
    xt = np.ascontiguousarray(np.asarray(x_c, np.float32).T)
    return np.ascontiguousarray(xt.reshape(8, 128, T).transpose(1, 0, 2).reshape(128, 8 * T))


def host_out_core(o, T):
    return np.ascontiguousarray(o.reshape(128, 8, T).transpose(1, 0, 2).reshape(1024, T).T)



def build_program(T=2048, upto="p11", taps=()):
    assert T % 512 == 0
    NT = T // 512
    NTT = T // 128
    C1 = 256
    NCH = T // C1
    TCH = max(512, T // 2)
    NTCH = T // TCH
    NCOL = TCH // 512
    NG = NTT // 2

    PH = ["p0", "p1", "p2", "p2a", "p3", "p4", "p5", "p6", "p7", "p8", "p9", "p10", "p11"]
    stop_i = PH.index(upto)
    act = lambda name: PH.index(name) <= stop_i

    nc = bass.Bass("TRN2", target_bir_lowering=False)
    st = ExitStack()

    dram = {"x": nc.dram_tensor("x", [128, NFC * T], F32, kind="ExternalInput").ap()}
    for nm, shp, dt in [
        ("wq0", [8, 128, 1024], F32), ("wk0", [8, 128, 1024], F32), ("wv0", [8, 128, 1024], F32),
        ("bq0", [128, 8], F32), ("bk0", [128, 8], F32), ("bv0", [1, 1024], F32),
        ("wo0", [128, 8 * 1024], F32), ("bo0", [128, 8], F32),
        ("w1", [NFC, 128, FF], F32), ("b1", [128, FF // 128], F32),
        ("w2", [FF // 128, 128, D], F32), ("b2", [128, 8], F32),
        ("wq1", [128, 8 * 256], F32), ("wk1", [128, 8 * 256], F32),
        ("wv1", [128, 8 * 1024], F32), ("wo1", [128, 8 * 1024], F32),
        ("wg1", [6, 8, 128, 512], I32), ("wu1", [6, 8, 128, 512], I32),
        ("wd1", [8, 24, 128, 128], I32),
    ]:
        dram[nm] = nc.dram_tensor(nm, shp, dt, kind="ExternalInput").ap()
    dram["out"] = nc.dram_tensor("out", [128, NFC * T], F32, kind="ExternalOutput").ap()
    dram["w1b"] = nc.dram_tensor("w1b", [NFC, 128, FF], BF16, kind="Internal").ap()
    dram["w2b"] = nc.dram_tensor("w2b", [FF // 128, 128, D], BF16, kind="Internal").ap()

    tap_outs = {}
    for tp in taps:
        shp = {"xln": [128, NFC * T], "y0": [128, NFC * T], "x1": [128, NFC * T],
               "x2": [128, NFC * T], "x3": [128, NFC * T], "y1": [128, NFC * T],
               "qf1": [128, 2 * T], "kf1": [128, 2 * T],
               "qtok": [128, NTT * 256], "ktok": [128, NTT * 256],
               "vtok": [128, NTT * 1040]}[tp]
        dt = F32 if tp in ("x1", "x2", "x3") else BF16
        tap_outs[tp] = nc.dram_tensor("tap_" + tp, shp, dt, kind="ExternalOutput").ap()

    def tap(name, tensor):
        if name not in tap_outs:
            return
        p = Ph()
        p.dma("sync", tap_outs[name], tensor[:, :], dout)
        p.wait("sync", dout)
        p.emit(nc)

    dc = Sem(st, nc, "dc")
    dwp = [Sem(st, nc, f"dwp{i}") for i in range(2)]
    dsl = [Sem(st, nc, f"dsl{i}") for i in range(2)]
    dst = [Sem(st, nc, f"dst{i}") for i in range(2)]
    dw1 = [Sem(st, nc, f"dw1{i}") for i in range(2)]
    dw2 = [Sem(st, nc, f"dw2{i}") for i in range(2)]
    dq1 = Sem(st, nc, "dq1")
    dtr = [Sem(st, nc, f"dtr{i}") for i in range(3)]
    dgu = [Sem(st, nc, f"dgu{i}") for i in range(2)]
    dwd = [Sem(st, nc, f"dwd{i}") for i in range(2)]
    dout = Sem(st, nc, "dout")
    dout2 = Sem(st, nc, "dout2")
    ts = Sem(st, nc, "ts")
    ss = Sem(st, nc, "ss")
    vs = Sem(st, nc, "vs")

    sbuf = lambda name, shp, dt, stk=st: stk.enter_context(nc.sbuf_tensor(name, shp, dt))

    X = sbuf("X", [128, NFC * T], F32)
    ONES_B = sbuf("ONES_B", [128, 128], BF16)
    ONES_F = sbuf("ONES_F", [128, 128], F32)
    NRM2 = sbuf("NRM2", [128, 2], BF16)
    TRIL = sbuf("TRIL", [128, 128], BF16)
    IOTA = sbuf("IOTA", [128, 128], F32)
    BQ = sbuf("BQ", [128, 8], F32)
    BQ8 = sbuf("BQ8", [128, 8], F32)
    BK = sbuf("BK", [128, 8], F32)
    BVR = sbuf("BVR", [1, 1024], BF16)
    BO = sbuf("BO", [128, 8], F32)
    B1 = sbuf("B1", [128, FF // 128], F32)
    B2 = sbuf("B2", [128, 8], F32)
    CB = sbuf("CB", [128, 3], F32)
    HMASK = sbuf("HMASK", [128, 8], F32)
    IOTA16 = sbuf("IOTA16", [128, 8], F32)
    IDN = sbuf("IDN", [128, 128], BF16)

    _ln_ctr = [0]

    def ln_phase(XLN):
        _ln_ctr[0] += 1
        sfx = str(_ln_ctr[0])
        with ExitStack() as ps:
            XB = ps.enter_context(nc.sbuf_tensor("lnXB" + sfx, [128, NFC * 512], BF16))
            XSQ = ps.enter_context(nc.sbuf_tensor("lnXSQ" + sfx, [128, NFC * 512], BF16))
            TMPA = ps.enter_context(nc.sbuf_tensor("lnTMP" + sfx, [128, NFC * 512], BF16))
            ROWS = ps.enter_context(nc.sbuf_tensor("lnROWS" + sfx, [1, 7 * 512], F32))
            RBF = ps.enter_context(nc.sbuf_tensor("lnRBF" + sfx, [1, 2 * 512], BF16))
            RB = ps.enter_context(nc.sbuf_tensor("lnRB" + sfx, [128, 512], BF16))
            MRB = ps.enter_context(nc.sbuf_tensor("lnMRB" + sfx, [128, 512], BF16))
            psA = ps.enter_context(nc.psum_tensor("lnA" + sfx, [1, 512], F32))
            psB = ps.enter_context(nc.psum_tensor("lnB" + sfx, [1, 512], F32))
            psR = ps.enter_context(nc.psum_tensor("lnR" + sfx, [128, 512], F32))
            psM = ps.enter_context(nc.psum_tensor("lnM" + sfx, [128, 512], F32))
            sS, s2S = ROWS[:, 0:512], ROWS[:, 512:1024]
            uS, wS = ROWS[:, 1024:1536], ROWS[:, 1536:2048]
            lnwS, rS, mrS = ROWS[:, 2048:2560], ROWS[:, 2560:3072], ROWS[:, 3072:3584]

            p = Ph()
            prev_apply_vs = 0
            prev_stat_read_vs = 0
            prev_bcast_evac_ss = 0
            for tc in range(NT):
                p.wait("scalar", vs, prev_apply_vs)
                for c in range(NFC):
                    p.op("scalar", "activation", XB[:, c * 512:(c + 1) * 512],
                         X[:, c * T + tc * 512: c * T + tc * 512 + 512], AF.Copy,
                         inc=(ss if c == NFC - 1 else None))
                cast_ss = ss.n
                p.wait("vector", ss, cast_ss)
                for c in range(NFC):
                    p.op("vector", "tensor_mul", XSQ[:, c * 512:(c + 1) * 512],
                         XB[:, c * 512:(c + 1) * 512], XB[:, c * 512:(c + 1) * 512],
                         inc=(vs if c == NFC - 1 else None))
                sq_vs = vs.n
                p.wait("tensor", ss, cast_ss)
                p.wait("tensor", vs, prev_stat_read_vs)
                for c in range(NFC):
                    p.op("tensor", "matmul", psA[0:1, :], ONES_B[:, 0:1],
                         XB[:, c * 512:(c + 1) * 512], start=(c == 0), stop=(c == NFC - 1),
                         inc=(ts if c == NFC - 1 else None))
                tsA = ts.n
                p.wait("tensor", vs, sq_vs)
                for c in range(NFC):
                    p.op("tensor", "matmul", psB[0:1, :], ONES_B[:, 0:1],
                         XSQ[:, c * 512:(c + 1) * 512], start=(c == 0), stop=(c == NFC - 1),
                         inc=(ts if c == NFC - 1 else None))
                tsB = ts.n
                p.wait("vector", ts, tsA)
                p.op("vector", "tensor_copy", sS, psA[0:1, :])
                p.wait("vector", ts, tsB)
                p.op("vector", "tensor_copy", s2S, psB[0:1, :])
                p.op("vector", "drain")
                p.op("vector", "tensor_mul", uS, sS, sS)
                p.op("vector", "drain")
                p.op("vector", "scalar_tensor_tensor", wS, s2S, 1024.0, uS,
                     AL.mult, AL.subtract, inc=vs)
                prev_stat_read_vs = vs.n
                p.wait("scalar", vs, prev_stat_read_vs)
                p.op("scalar", "activation", lnwS, wS, AF.Ln, scale=2.0 ** -20,
                     bias=CB[0:1, 0:1])
                p.op("scalar", "drain")
                p.op("scalar", "activation", rS, lnwS, AF.Exp, scale=-0.5, inc=ss)
                p.wait("vector", ss)
                p.op("vector", "tensor_copy", RBF[:, 0:512], rS)
                p.op("vector", "scalar_tensor_tensor", mrS, sS, 1.0 / 1024.0, rS,
                     AL.mult, AL.mult)
                p.op("vector", "drain")
                p.op("vector", "tensor_copy", RBF[:, 512:1024], mrS, inc=vs)
                p.wait("tensor", vs)
                p.wait("tensor", ss, prev_bcast_evac_ss)
                p.op("tensor", "matmul", psR[:, :], ONES_B[0:1, :], RBF[:, 0:512],
                     start=True, stop=True)
                p.op("tensor", "matmul", psM[:, :], ONES_B[0:1, :], RBF[:, 512:1024],
                     start=True, stop=True, inc=ts)
                p.wait("scalar", ts)
                p.wait("scalar", vs, prev_apply_vs)
                p.op("scalar", "activation", RB[:, :], psR[:, :], AF.Copy)
                p.op("scalar", "activation", MRB[:, :], psM[:, :], AF.Copy, inc=ss)
                prev_bcast_evac_ss = ss.n
                p.wait("vector", ss)
                for c in range(NFC):
                    p.op("vector", "tensor_mul", TMPA[:, c * 512:(c + 1) * 512],
                         XB[:, c * 512:(c + 1) * 512], RB[:, :])
                p.op("vector", "drain")
                for c in range(NFC):
                    p.op("vector", "tensor_sub",
                         XLN[:, c * T + tc * 512: c * T + tc * 512 + 512],
                         TMPA[:, c * 512:(c + 1) * 512], MRB[:, :],
                         inc=(vs if c == NFC - 1 else None))
                prev_apply_vs = vs.n
            p.emit(nc)

    es0 = ExitStack()
    XLN = sbuf("XLN0", [128, NFC * T], BF16, es0)
    es0a = ExitStack()
    Y = sbuf("Y0", [128, NFC * T], BF16, es0a)
    WQP = [sbuf(f"WQP{i}", [128, 1024], BF16, es0a) for i in range(2)]
    WKP = [sbuf(f"WKP{i}", [128, 1024], BF16, es0a) for i in range(2)]
    WVP = [sbuf(f"WVP{i}", [128, 1024], BF16, es0a) for i in range(2)]

    p = Ph()
    p.dma("gpsimd", X[:, :], dram["x"], dc)
    p.dma("gpsimd", WQP[0][:, :], dram["wq0"][0], dc)
    p.dma("gpsimd", WKP[0][:, :], dram["wk0"][0], dc)
    p.dma("gpsimd", WVP[0][:, :], dram["wv0"][0], dc)
    p.dma("gpsimd", BQ[:, :], dram["bq0"], dc)
    p.dma("gpsimd", BK[:, :], dram["bk0"], dc)
    p.dma("gpsimd", BVR[:, :], dram["bv0"], dc)
    p.dma("gpsimd", BO[:, :], dram["bo0"], dc)
    p.dma("gpsimd", B1[:, :], dram["b1"], dc)
    p.dma("gpsimd", B2[:, :], dram["b2"], dc)
    wpair_done = [(dc, dc.n)]
    p.add("gpsimd", lambda g: g.iota(IOTA[:, :], [[1, 128]], channel_multiplier=-1,
                                     allow_small_or_imprecise_dtypes=True))
    p.add("gpsimd", lambda g: g.iota(IOTA16[:, :], [[16, 8]], channel_multiplier=-1,
                                     allow_small_or_imprecise_dtypes=True))
    p.add("vector", lambda v: v.memset(ONES_B[:, :], 1.0))
    p.add("vector", lambda v: v.memset(ONES_F[:, :], 1.0))
    p.add("vector", lambda v: v.memset(NRM2[:, :], 0.0))
    p.add("vector", lambda v: v.drain())
    p.add("vector", lambda v: v.memset(NRM2[0:64, 0:1], 1.0))
    p.add("vector", lambda v: v.memset(NRM2[64:128, 1:2], 1.0))
    p.add("vector", lambda v: v.memset(CB[:, 0:1], 1e-5))
    p.add("vector", lambda v: v.memset(CB[:, 2:3], 0.5))
    p.emit(nc)

    p = Ph()
    p.add("vector", lambda v: v.tensor_single_scalar(TRIL[:, :], IOTA[:, :], 0.0, AL.is_ge))
    p.add("vector", lambda v: v.tensor_single_scalar(IDN[:, :], IOTA[:, :], 0.0, AL.is_equal))
    p.add("vector", lambda v: v.tensor_single_scalar(HMASK[:, :], IOTA16[:, :], 0.0, AL.is_le))
    p.add("vector", lambda v: v.tensor_single_scalar(IOTA16[:, :], IOTA16[:, :], -16.0, AL.is_gt))
    p.add("vector", lambda v: v.drain())
    p.add("vector", lambda v: v.tensor_mul(HMASK[:, :], HMASK[:, :], IOTA16[:, :]))
    p.wait("vector", dc)
    p.add("vector", lambda v: v.tensor_scalar_mul(BQ8[:, :], BQ[:, :], 0.125))
    p.emit(nc)

    if act("p0"):
        ln_phase(XLN)
    tap("xln", XLN)

    if act("p1"):
        with ExitStack() as ps:
            QP = [ps.enter_context(nc.sbuf_tensor(f"QP{i}", [128, T], BF16)) for i in range(2)]
            KP = [ps.enter_context(nc.sbuf_tensor(f"KP{i}", [128, T], BF16)) for i in range(2)]
            VP = [ps.enter_context(nc.sbuf_tensor(f"VP{i}", [128, NTT * 130], BF16)) for i in range(2)]
            SCR = ps.enter_context(nc.sbuf_tensor("SCR", [128, 2048 + 2 * T], BF16))
            PSB = [SCR[:, 0:1024], SCR[:, 1024:2048]]
            QS, KS = SCR[:, 2048:2048 + T], SCR[:, 2048 + T:2048 + 2 * T]
            NRED = ps.enter_context(nc.sbuf_tensor("NRED", [1, 4 * NT], F32))
            MXR = ps.enter_context(nc.sbuf_tensor("MXR", [1, 8], F32))
            NEGC1 = ps.enter_context(nc.sbuf_tensor("NEGC1", [1, 1], F32))
            NEGC = [ps.enter_context(nc.sbuf_tensor(f"NEGCB{i}", [128, 1], F32)) for i in range(2)]
            DROW = ps.enter_context(nc.sbuf_tensor("DROW", [65, NT * 1024], BF16))
            WSL = [ps.enter_context(nc.sbuf_tensor(f"WSL{i}", [128, 1024], F32)) for i in range(2)]
            WTB = [ps.enter_context(nc.sbuf_tensor(f"WTB{i}", [128, 1024], BF16)) for i in range(2)]
            TMP1 = ps.enter_context(nc.sbuf_tensor("terTMP", [128, 1024], F32))
            RACC = ps.enter_context(nc.sbuf_tensor("terRACC", [128, 64], F32))
            RS1 = ps.enter_context(nc.sbuf_tensor("terRS1", [128, 1], F32))
            THRR = ps.enter_context(nc.sbuf_tensor("terTHRR", [1, 2], F32))
            THRV = ps.enter_context(nc.sbuf_tensor("terTHRV", [128, 2], F32))
            S4 = ps.enter_context(nc.psum_tensor("S4", [128, 2048], F32))
            PO = ps.enter_context(nc.psum_tensor("PO", [65, 1024], F32))
            PAUXT = ps.enter_context(nc.psum_tensor("PAUX", [128, 1024], F32))
            PAX = [PAUXT[:, 0:512], PAUXT[:, 512:1024]]

            NKT = T // 128
            p = Ph()
            for i in range(2):
                p.add("vector", lambda v, i=i: v.memset(
                    VP[i].rearrange("p (t f) -> p t f", t=NTT)[:, :, 64:65], 1.0))
                p.add("vector", lambda v, i=i: v.memset(
                    VP[i].rearrange("p (t f) -> p t f", t=NTT)[:, :, 129:130], 1.0))
            p.op("vector", "drain", inc=vs)

            trk = {"pax_free": [vs.n, vs.n],
                   "pax_i": 0,
                   "wread_ts": [0, 0],
                   "prep_done": [vs.n, vs.n],
                   "attn_ts": [0] * 10,
                   "last_exp_ss": 0,
                   "o_evac_vs": 0,
                   "zmul_vs": 0}
            w12b_done = []

            def prep_chunks(pr):
                pb = pr % 2
                if pr + 1 < 8:
                    p.wait("gpsimd", ts, trk["wread_ts"][1 - pb])
                    p.dma("gpsimd", WQP[1 - pb][:, :], dram["wq0"][pr + 1], dwp[1 - pb])
                    p.dma("gpsimd", WKP[1 - pb][:, :], dram["wk0"][pr + 1], dwp[1 - pb])
                    p.dma("gpsimd", WVP[1 - pb][:, :], dram["wv0"][pr + 1], dwp[1 - pb])
                    wpair_done.append((dwp[1 - pb], dwp[1 - pb].n))
                wsem, wval = wpair_done[pr]
                p.wait("tensor", wsem, wval)
                if pr >= 2:
                    p.wait("vector", ts, trk["attn_ts"][pr - 2])
                for which in range(2):
                    WB = WQP[pb] if which == 0 else WKP[pb]
                    DSTT = QP[pb] if which == 0 else KP[pb]
                    for tc in range(NT):
                        px = trk["pax_i"] = 1 - trk["pax_i"]
                        p.wait("tensor", vs, trk["pax_free"][px])
                        for c in range(NFC):
                            p.op("tensor", "matmul", PAX[px][:, :], WB[:, c * 128:(c + 1) * 128],
                                 XLN[:, c * T + tc * 512: c * T + tc * 512 + 512],
                                 start=(c == 0), stop=(c == NFC - 1),
                                 inc=(ts if c == NFC - 1 else None))
                        p.wait("vector", ts)
                        if which == 0:
                            p.op("vector", "tensor_scalar", DSTT[:, tc * 512:(tc + 1) * 512],
                                 PAX[px][:, :], 0.125, BQ8[:, pr:pr + 1], AL.mult, AL.add,
                                 inc=vs)
                        else:
                            p.op("vector", "tensor_copy", DSTT[:, tc * 512:(tc + 1) * 512],
                                 PAX[px][:, :], inc=vs)
                        trk["pax_free"][px] = vs.n
                        yield pr
                for tt in range(NTT):
                    px = trk["pax_i"] = 1 - trk["pax_i"]
                    p.wait("tensor", vs, trk["pax_free"][px])
                    p.op("tensor", "matmul", PAX[px][:, 0:128], ONES_B[0:1, 0:128],
                         BVR[:, pr * 128:(pr + 1) * 128],
                         start=True, stop=False)
                    for c in range(NFC):
                        p.op("tensor", "matmul", PAX[px][:, 0:128],
                             XLN[:, c * T + tt * 128: c * T + tt * 128 + 128],
                             WVP[pb][:, c * 128:(c + 1) * 128],
                             start=False, stop=(c == NFC - 1),
                             inc=(ts if c == NFC - 1 else None))
                    p.wait("vector", ts)
                    p.op("vector", "tensor_copy",
                         VP[pb].rearrange("p (t b f) -> p t b f", t=NTT, b=2)[:, tt, :, 0:64],
                         PAX[px][:, 0:128].rearrange("p (b f) -> p b f", b=2), inc=vs)
                    trk["pax_free"][px] = vs.n
                    if tt % 2 == 1:
                        yield pr
                trk["wread_ts"][pb] = ts.n
                p.wait("vector", vs, trk["prep_done"][pb])
                p.op("vector", "tensor_mul", QS, QP[pb][:, :], QP[pb][:, :])
                p.op("vector", "tensor_mul", KS, KP[pb][:, :], KP[pb][:, :], inc=vs)
                sq_vs = vs.n
                yield pr
                p.wait("tensor", vs, sq_vs)
                for which, SRC in enumerate([QS, KS]):
                    for aa in range(2):
                        for tc in range(NT):
                            g = (which * 2 + aa) * NT + tc
                            px = trk["pax_i"] = 1 - trk["pax_i"]
                            p.wait("tensor", vs, trk["pax_free"][px])
                            p.op("tensor", "matmul", PAX[px][0:1, 0:512], NRM2[:, aa:aa + 1],
                                 SRC[:, tc * 512:(tc + 1) * 512], start=True, stop=True,
                                 inc=ts)
                            p.wait("vector", ts)
                            p.op("vector", "reduce_max", NRED[0:1, g:g + 1], PAX[px][0:1, 0:512],
                                 axis=AX.X, inc=vs)
                            trk["pax_free"][px] = vs.n
                        yield pr
                p.op("vector", "drain")
                for j in range(4):
                    p.op("vector", "reduce_max", MXR[0:1, j:j + 1],
                         NRED[0:1, j * NT:(j + 1) * NT], axis=AX.X)
                p.op("vector", "drain")
                p.op("vector", "tensor_mul", MXR[0:1, 4:5], MXR[0:1, 0:1], MXR[0:1, 2:3])
                p.op("vector", "tensor_mul", MXR[0:1, 5:6], MXR[0:1, 1:2], MXR[0:1, 3:4])
                p.op("vector", "drain")
                p.op("vector", "reduce_max", MXR[0:1, 6:7], MXR[0:1, 4:6], axis=AX.X)
                p.op("vector", "drain", inc=vs)
                yield pr
                p.wait("scalar", vs)
                p.op("scalar", "activation", MXR[0:1, 7:8], MXR[0:1, 6:7], AF.Ln)
                p.op("scalar", "drain")
                p.op("scalar", "activation", NEGC1[:, :], MXR[0:1, 7:8], AF.Exp,
                     scale=0.5, inc=ss)
                p.wait("vector", ss)
                p.op("vector", "tensor_scalar_mul", NEGC1[:, :], NEGC1[:, :], -1.0)
                p.op("vector", "drain", inc=vs)
                yield pr
                p.wait("tensor", vs)
                p.wait("tensor", vs, trk["pax_free"][0])
                p.op("tensor", "matmul", PAX[0][:, 1:2], ONES_F[0:1, 0:128], NEGC1[:, :],
                     start=True, stop=True, inc=ts)
                p.wait("vector", ts)
                p.op("vector", "tensor_copy", NEGC[pb][:, :], PAX[0][:, 1:2], inc=vs)
                trk["pax_free"][0] = vs.n
                trk["prep_done"][pb] = vs.n
                yield pr

            def ternary_chunks():
                w1q = dram["w1"].rearrange("c p (q f) -> c q p f", f=1024)
                w1bq = dram["w1b"].rearrange("c p (q f) -> c q p f", f=1024)
                for wdram, wbdram in [(lambda i: w1q[i // 4, i % 4], lambda i: w1bq[i // 4, i % 4]),
                                      (lambda i: dram["w2"][i], lambda i: dram["w2b"][i])]:
                    nslab = 32
                    red_v0 = vs.n
                    p.wait("gpsimd", vs)
                    p.dma("gpsimd", WSL[0][:, :], wdram(0), dsl[0])
                    p.dma("gpsimd", WSL[1][:, :], wdram(1), dsl[1])
                    for s_ in range(nslab):
                        yield 99
                        yield 99
                        p.wait("vector", dsl[s_ % 2])
                        p.op("vector", "reduce_sum", RACC[:, s_:s_ + 1], WSL[s_ % 2][:, :],
                             axis=AX.X, apply_absolute_value=True, inc=vs)
                        if s_ + 2 < nslab:
                            p.wait("gpsimd", vs)
                            p.dma("gpsimd", WSL[s_ % 2][:, :], wdram(s_ + 2), dsl[s_ % 2])
                        yield 99
                    p.op("vector", "drain")
                    p.op("vector", "reduce_sum", RS1[:, 0:1], RACC[:, 0:nslab], axis=AX.X,
                         inc=vs)
                    yield 99
                    p.wait("tensor", vs)
                    p.wait("tensor", vs, trk["pax_free"][0])
                    p.op("tensor", "matmul", PAX[0][0:1, 2:3], ONES_F[:, 0:1], RS1[:, 0:1],
                         start=True, stop=True, inc=ts)
                    p.wait("vector", ts)
                    p.op("vector", "tensor_scalar", THRR[:, 0:1], PAX[0][0:1, 2:3],
                         1.0 / (nslab * 128.0 * 1024.0), 1e-8, AL.mult, AL.max)
                    p.op("vector", "drain")
                    p.op("vector", "tensor_scalar_mul", THRR[:, 0:1], THRR[:, 0:1], 0.5)
                    p.op("vector", "drain")
                    p.op("vector", "tensor_scalar_mul", THRR[:, 1:2], THRR[:, 0:1], -1.0)
                    p.op("vector", "drain", inc=vs)
                    yield 99
                    p.wait("tensor", vs)
                    p.op("tensor", "matmul", PAX[0][:, 3:5], ONES_F[0:1, 0:128], THRR[:, 0:2],
                         start=True, stop=True, inc=ts)
                    p.wait("vector", ts)
                    p.op("vector", "tensor_copy", THRV[:, :], PAX[0][:, 3:5])
                    p.op("vector", "drain", inc=vs)
                    trk["pax_free"][0] = vs.n
                    yield 99
                    ter_v0 = vs.n
                    p.wait("vector", dst[0])
                    p.wait("vector", dst[1])
                    p.wait("gpsimd", vs)
                    p.dma("gpsimd", WSL[0][:, :], wdram(0), dsl[0])
                    p.dma("gpsimd", WSL[1][:, :], wdram(1), dsl[1])
                    for s_ in range(nslab):
                        yield 99
                        yield 99
                        p.wait("vector", dsl[s_ % 2])
                        if s_ >= 2:
                            p.wait("vector", dst[s_ % 2])
                        p.op("vector", "tensor_single_scalar", TMP1[:, :], WSL[s_ % 2][:, :],
                             THRV[:, 1:2], AL.is_lt)
                        p.op("vector", "drain")
                        p.op("vector", "scalar_tensor_tensor", WTB[s_ % 2][:, :],
                             WSL[s_ % 2][:, :], THRV[:, 0:1], TMP1[:, :],
                             AL.is_gt, AL.subtract)
                        p.op("vector", "drain", inc=vs)
                        p.wait("sync", vs)
                        p.dma("sync", wbdram(s_), WTB[s_ % 2][:, :], dst[s_ % 2])
                        if s_ + 2 < nslab:
                            p.wait("gpsimd", vs)
                            p.dma("gpsimd", WSL[s_ % 2][:, :], wdram(s_ + 2), dsl[s_ % 2])
                        yield 99
                w12b_done.append((dst[0], dst[0].n))
                w12b_done.append((dst[1], dst[1].n))
                yield 99

            tern = ternary_chunks()
            tern_done = [False]
            cur_prep = [None]

            def pump(n=1):
                for _ in range(n):
                    if cur_prep[0] is not None:
                        try:
                            next(cur_prep[0])
                            continue
                        except StopIteration:
                            cur_prep[0] = None
                    if not tern_done[0]:
                        try:
                            next(tern)
                        except StopIteration:
                            tern_done[0] = True

            for _ in prep_chunks(0):
                pass

            for pr in range(8):
                pb = pr % 2
                cur_prep[0] = prep_chunks(pr + 1) if pr + 1 < 8 else None
                p.wait("tensor", vs, trk["prep_done"][pb])
                p.wait("tensor", ss, trk["last_exp_ss"])
                for qc in range(NT):
                    pv_ts = {}
                    exp_ss = {}
                    for kt in range(NKT + 1):
                        if kt < NKT:
                            prev = exp_ss.get(kt - 2, trk["last_exp_ss"])
                            p.wait("tensor", ss, prev)
                            sb = (kt % 2) * 1024
                            for aa in range(2):
                                p.op("tensor", "matmul",
                                     S4[:, sb + aa * 512: sb + aa * 512 + 512],
                                     KP[pb][64 * aa:64 * aa + 64, kt * 128:(kt + 1) * 128],
                                     QP[pb][64 * aa:64 * aa + 64, qc * 512:(qc + 1) * 512],
                                     start=True, stop=True,
                                     inc=(ts if aa == 1 else None))
                            s2_ts = ts.n
                        if kt > 0:
                            pk = kt - 1
                            p.wait("tensor", ss, exp_ss[pk])
                            if pk == 0:
                                p.wait("tensor", vs, trk["o_evac_vs"])
                                p.wait("tensor", vs, trk["zmul_vs"])
                            for aa in range(2):
                                p.op("tensor", "matmul", PO[:, aa * 512:(aa + 1) * 512],
                                     VP[pb][:, pk * 130 + aa * 65: pk * 130 + aa * 65 + 65],
                                     PSB[pk % 2][:, aa * 512:(aa + 1) * 512],
                                     start=(pk == 0), stop=(pk == NKT - 1),
                                     skip_group_check=True,
                                     inc=(ts if aa == 1 else None))
                            pv_ts[pk] = ts.n
                        if kt < NKT:
                            p.wait("scalar", ts, pv_ts.get(kt - 2, 0))
                            p.wait("scalar", ts, s2_ts)
                            p.op("scalar", "activation", PSB[kt % 2][:, :],
                                 S4[:, (kt % 2) * 1024:(kt % 2) * 1024 + 1024], AF.Exp,
                                 bias=NEGC[pb][:, 0:1], inc=ss)
                            exp_ss[(kt)] = ss.n
                        pump()
                    trk["last_exp_ss"] = exp_ss[NKT - 1]
                    o_ts = ts.n
                    p.wait("vector", ts, o_ts)
                    for aa in range(2):
                        p.op("vector", "tensor_copy",
                             Y[64 * aa:64 * aa + 64, pr * T + qc * 512: pr * T + qc * 512 + 512],
                             PO[0:64, aa * 512: aa * 512 + 512])
                        p.op("vector", "tensor_copy",
                             DROW[64:65, (qc * 2 + aa) * 512: (qc * 2 + aa) * 512 + 512],
                             PO[64:65, aa * 512: aa * 512 + 512],
                             inc=(vs if aa == 1 else None))
                    trk["o_evac_vs"] = vs.n
                trk["attn_ts"][pr] = ts.n
                p.op("vector", "drain")
                for g in range(NT * 2):
                    p.op("vector", "reciprocal", DROW[64:65, g * 512:(g + 1) * 512],
                         DROW[64:65, g * 512:(g + 1) * 512], lowp=True,
                         inc=(vs if g == NT * 2 - 1 else None))
                zr_vs = vs.n
                for qc in range(NT):
                    for aa in range(2):
                        g = qc * 2 + aa
                        p.wait("tensor", vs, zr_vs)
                        p.wait("tensor", vs, trk["zmul_vs"])
                        p.op("tensor", "matmul", PO[0:64, 0:512], ONES_B[64:65, 0:64],
                             DROW[64:65, g * 512:(g + 1) * 512], start=True, stop=True,
                             skip_group_check=True, inc=ts)
                        zb_t = ts.n
                        p.wait("vector", ts, zb_t)
                        yslc = Y[64 * aa:64 * aa + 64, pr * T + qc * 512: pr * T + qc * 512 + 512]
                        p.op("vector", "tensor_mul", yslc, yslc, PO[0:64, 0:512], inc=vs)
                        trk["zmul_vs"] = vs.n
                        pump()
                while cur_prep[0] is not None:
                    pump()
            while cur_prep[0] is not None or not tern_done[0]:
                pump()
            p.emit(nc)
    tap("y0", Y)

    if act("p2"):
        with ExitStack() as ps:
            WO0 = ps.enter_context(nc.sbuf_tensor("WO0", [128, 8 * 1024], BF16))
            PWO = ps.enter_context(nc.psum_tensor("PWO", [128, 1024], F32))
            T1 = ps.enter_context(nc.sbuf_tensor("T1", [128, 1024], F32))
            p = Ph()
            p.dma("gpsimd", WO0[:, :], dram["wo0"], dc)
            p.wait("tensor", dc)
            wo_ev0 = ss.n
            add_v0 = vs.n
            for g in range(NFC * NT):
                fc, tc = divmod(g, NT)
                if g >= 2:
                    p.wait("tensor", ss, wo_ev0 + g - 1)
                pw = PWO[:, (g % 2) * 512:(g % 2) * 512 + 512]
                for c in range(NFC):
                    p.op("tensor", "matmul", pw,
                         WO0[:, c * 1024 + fc * 128: c * 1024 + fc * 128 + 128],
                         Y[:, c * T + tc * 512: c * T + tc * 512 + 512],
                         start=(c == 0), stop=(c == NFC - 1),
                         inc=(ts if c == NFC - 1 else None))
                p.wait("scalar", ts)
                if g >= 2:
                    p.wait("scalar", vs, add_v0 + g - 1)
                p.op("scalar", "activation", T1[:, (g % 2) * 512:(g % 2) * 512 + 512], pw,
                     AF.Identity, bias=BO[:, fc:fc + 1], inc=ss)
                p.wait("vector", ss)
                p.op("vector", "tensor_add",
                     X[:, fc * T + tc * 512: fc * T + tc * 512 + 512],
                     X[:, fc * T + tc * 512: fc * T + tc * 512 + 512],
                     T1[:, (g % 2) * 512:(g % 2) * 512 + 512], inc=vs)
            p.emit(nc)
    es0a.close()
    tap("x1", X)

    if act("p3"):
        ln_phase(XLN)

    if act("p4"):
        with ExitStack() as ps:
            HSB = ps.enter_context(nc.sbuf_tensor("HSB", [128, (FF // 128) * TCH], BF16))
            W1S = [ps.enter_context(nc.sbuf_tensor(f"W1S{i}", [128, NFC * 512], BF16)) for i in range(2)]
            W2S = [ps.enter_context(nc.sbuf_tensor(f"W2S{i}", [128, (FF // 128) * 128], BF16)) for i in range(2)]
            TF1 = ps.enter_context(nc.sbuf_tensor("TF1", [128, 1024], F32))
            PH1 = ps.enter_context(nc.psum_tensor("PH1", [128, 2048], F32))
            w1v = dram["w1b"].rearrange("c p (b f) -> b c p f", f=512)
            w2v = dram["w2b"].rearrange("k p (b f) -> b k p f", f=128)
            NFB = FF // 512
            p = Ph()
            w1s_done, w2s_done = [], []
            for sm, v in w12b_done:
                p.wait("sync", sm, v)
            for i in range(2):
                p.dma("sync", W1S[i].rearrange("p (c f) -> p c f", f=512), w1v[i].rearrange("c p f -> p c f"), dw1[i])
                w1s_done.append((dw1[i], dw1[i].n))
                p.dma("sync", W2S[i].rearrange("p (k f) -> p k f", f=128), w2v[i].rearrange("k p f -> p k f"), dw2[i])
                w2s_done.append((dw2[i], dw2[i].n))
            h_mm_ts = {}
            for tch in range(NTCH):
                p.wait("tensor", ss)
                gel_s0 = ss.n
                for fb in range(NFB):
                    gi = tch * NFB + fb
                    p.wait("tensor", *w1s_done[gi])
                    for fi in range(4):
                        for col in range(NCOL):
                            g = (fb * 4 + fi) * NCOL + col
                            if g >= 4:
                                p.wait("tensor", ss, gel_s0 + g - 3)
                            pq = PH1[:, (g % 4) * 512:(g % 4) * 512 + 512]
                            for c in range(NFC):
                                p.op("tensor", "matmul", pq,
                                     W1S[fb % 2][:, c * 512 + fi * 128: c * 512 + fi * 128 + 128],
                                     XLN[:, c * T + tch * TCH + col * 512: c * T + tch * TCH + col * 512 + 512],
                                     start=(c == 0), stop=(c == NFC - 1),
                                     inc=(ts if c == NFC - 1 else None))
                            p.wait("scalar", ts)
                            fidx = fb * 4 + fi
                            p.op("scalar", "activation",
                                 HSB[:, fidx * TCH + col * 512: fidx * TCH + col * 512 + 512],
                                 pq, AF.Gelu, bias=B1[:, fidx:fidx + 1], inc=ss)
                    h_mm_ts[gi] = ts.n
                    nxt = gi + 2
                    if nxt < NTCH * NFB:
                        p.wait("sync", ts, h_mm_ts[gi])
                        p.dma("sync", W1S[nxt % 2].rearrange("p (c f) -> p c f", f=512),
                              w1v[nxt % NFB].rearrange("c p f -> p c f"), dw1[nxt % 2])
                        w1s_done.append((dw1[nxt % 2], dw1[nxt % 2].n))
                h_done = ss.n
                p.wait("scalar", vs)
                add_v0 = vs.n
                dn_ev0 = ss.n
                for dcc in range(NFC):
                    gi = tch * NFC + dcc
                    p.wait("tensor", *w2s_done[gi])
                    p.wait("tensor", ss, h_done)
                    for col in range(NCOL):
                        g = dcc * NCOL + col
                        if g >= 2:
                            p.wait("tensor", ss, dn_ev0 + g - 1)
                        pq = PH1[:, (g % 2) * 512:(g % 2) * 512 + 512]
                        for k in range(FF // 128):
                            p.op("tensor", "matmul", pq,
                                 W2S[dcc % 2][:, k * 128:(k + 1) * 128],
                                 HSB[:, k * TCH + col * 512: k * TCH + col * 512 + 512],
                                 start=(k == 0), stop=(k == FF // 128 - 1),
                                 inc=(ts if k == FF // 128 - 1 else None))
                        p.wait("scalar", ts)
                        if g >= 2:
                            p.wait("scalar", vs, add_v0 + g - 1)
                        p.op("scalar", "activation", TF1[:, (g % 2) * 512:(g % 2) * 512 + 512],
                             pq, AF.Identity, bias=B2[:, dcc:dcc + 1], inc=ss)
                        p.wait("vector", ss)
                        p.op("vector", "tensor_add",
                             X[:, dcc * T + tch * TCH + col * 512: dcc * T + tch * TCH + col * 512 + 512],
                             X[:, dcc * T + tch * TCH + col * 512: dcc * T + tch * TCH + col * 512 + 512],
                             TF1[:, (g % 2) * 512:(g % 2) * 512 + 512], inc=vs)
                    dn_mm = ts.n
                    nxt = gi + 2
                    if nxt < NTCH * NFC:
                        p.wait("sync", ts, dn_mm)
                        p.dma("sync", W2S[nxt % 2].rearrange("p (k f) -> p k f", f=128),
                              w2v[nxt % NFC].rearrange("k p f -> p k f"), dw2[nxt % 2])
                        w2s_done.append((dw2[nxt % 2], dw2[nxt % 2].n))
            p.emit(nc)
    es0.close()
    tap("x2", X)

    if act("p5"):
        es1 = ExitStack()
        QF = sbuf("QF", [128, 2 * T], BF16, es1)
        KF = sbuf("KF", [128, 2 * T], BF16, es1)
        QTK = sbuf("QTK", [128, NTT * 256], BF16, es1)
        KTK = sbuf("KTK", [128, NTT * 256], BF16, es1)
        V1 = sbuf("V1", [128, NTT * 1040], BF16, es1)
        es1x = ExitStack()
        XLN = sbuf("XLN1", [128, NFC * T], BF16, es1x)

        ln_phase(XLN)

    if act("p6"):
        with ExitStack() as ps:
            WQ1 = ps.enter_context(nc.sbuf_tensor("WQ1", [128, 8 * 256], BF16))
            WK1 = ps.enter_context(nc.sbuf_tensor("WK1", [128, 8 * 256], BF16))
            WV1 = ps.enter_context(nc.sbuf_tensor("WV1", [128, 8 * 1024], BF16))
            PP1 = ps.enter_context(nc.psum_tensor("PP1", [128, 1024], F32))
            PV1 = ps.enter_context(nc.psum_tensor("PV1", [128, 1024], F32))
            p = Ph()
            p.dma("gpsimd", WQ1[:, :], dram["wq1"], dq1)
            p.dma("gpsimd", WK1[:, :], dram["wk1"], dq1)
            p.dma("gpsimd", WV1[:, :], dram["wv1"], dq1)
            wqkv1 = dq1.n
            p.add("vector", lambda v: v.memset(
                V1.rearrange("p (t h f) -> p t h f", t=NTT, h=16)[:, :, :, 64:65], 1.0))
            p.op("vector", "drain", inc=vs)
            v1_ones_vs = vs.n
            p.wait("tensor", dq1, wqkv1)
            ev0 = ss.n
            for which, (WB, DST) in enumerate([(WQ1, QF), (WK1, KF)]):
                for fc in range(2):
                    for tc in range(NT):
                        g = which * 2 * NT + fc * NT + tc
                        if g >= 2:
                            p.wait("tensor", ss, ev0 + g - 1)
                        pq = PP1[:, (g % 2) * 512:(g % 2) * 512 + 512]
                        for c in range(NFC):
                            p.op("tensor", "matmul", pq,
                                 WB[:, c * 256 + fc * 128: c * 256 + fc * 128 + 128],
                                 XLN[:, c * T + tc * 512: c * T + tc * 512 + 512],
                                 start=(c == 0), stop=(c == NFC - 1),
                                 inc=(ts if c == NFC - 1 else None))
                        p.wait("scalar", ts)
                        p.op("scalar", "activation",
                             DST[:, fc * T + tc * 512: fc * T + tc * 512 + 512], pq,
                             AF.Copy, inc=ss)
            p.wait("tensor", ss)
            ev1 = ss.n
            for which, (WB, DST) in enumerate([(WQ1, QTK), (WK1, KTK)]):
                kscl = S1 if which == 0 else S1 / math.sqrt(2.0)
                for tt in range(NTT):
                    g = which * NTT + tt
                    if g >= 2:
                        p.wait("tensor", ss, ev1 + g - 1)
                    pq = PP1[:, (g % 2) * 512: (g % 2) * 512 + 256]
                    for c in range(NFC):
                        p.op("tensor", "matmul", pq,
                             XLN[:, c * T + tt * 128: c * T + tt * 128 + 128],
                             WB[:, c * 256:(c + 1) * 256],
                             start=(c == 0), stop=(c == NFC - 1),
                             inc=(ts if c == NFC - 1 else None))
                    p.wait("scalar", ts)
                    p.op("scalar", "activation", DST[:, tt * 256:(tt + 1) * 256], pq,
                         AF.Identity, scale=kscl, inc=ss)
            p.wait("tensor", vs, v1_ones_vs)
            ev2 = vs.n
            for tt in range(NTT):
                for half in range(2):
                    g = tt * 2 + half
                    if g >= 2:
                        p.wait("tensor", vs, ev2 + g - 1)
                    pq = PV1[:, (g % 2) * 512:(g % 2) * 512 + 512]
                    for c in range(NFC):
                        p.op("tensor", "matmul", pq,
                             XLN[:, c * T + tt * 128: c * T + tt * 128 + 128],
                             WV1[:, c * 1024 + half * 512: c * 1024 + half * 512 + 512],
                             start=(c == 0), stop=(c == NFC - 1),
                             inc=(ts if c == NFC - 1 else None))
                    p.wait("vector", ts)
                    p.op("vector", "tensor_copy",
                         V1.rearrange("p (t h f) -> p t h f", t=NTT, h=16)[:, tt, half * 8:(half + 1) * 8, 0:64],
                         pq.rearrange("p (h f) -> p h f", h=8), inc=vs)
            p.emit(nc)
        tap("qf1", QF)
        tap("kf1", KF)
        tap("qtok", QTK)
        tap("ktok", KTK)
        tap("vtok", V1)
    if act("p5"):
        es1x.close()

    if act("p7"):
        es1y = ExitStack()
        Y = sbuf("Y1", [128, NFC * T], BF16, es1y)
        WO1 = sbuf("WO1b", [128, 8 * 1024], BF16, es1y)
        with ExitStack() as ps:
            KFB = [[ps.enter_context(nc.sbuf_tensor(f"KFB{i}_{j}", [128, 2 * 384], BF16))
                    for j in range(2)] for i in range(2)]
            QFTK = [ps.enter_context(nc.sbuf_tensor(f"QFTK{i}", [128, 2 * 384], BF16)) for i in range(2)]
            QFM = [ps.enter_context(nc.sbuf_tensor(f"QFM{i}", [128, 3 * 256], BF16)) for i in range(3)]
            RHQ = [ps.enter_context(nc.sbuf_tensor(f"RHQ{i}", [128, 256], BF16)) for i in range(2)]
            ARAW = [ps.enter_context(nc.sbuf_tensor(f"ARAW{i}", [128, 384], BF16)) for i in range(2)]
            ASB = [ps.enter_context(nc.sbuf_tensor(f"ASB{i}", [128, 384], BF16)) for i in range(2)]
            STSB = [ps.enter_context(nc.sbuf_tensor(f"STSB{i}", [128, 195], BF16)) for i in range(2)]
            Z1 = [ps.enter_context(nc.sbuf_tensor(f"Z1_{i}", [128, 256], BF16)) for i in range(2)]
            ZSC1 = [ps.enter_context(nc.sbuf_tensor(f"ZSC1_{i}", [65, 256], F32)) for i in range(2)]
            YT1 = [ps.enter_context(nc.sbuf_tensor(f"YT1_{i}", [64, 256], BF16)) for i in range(2)]
            PST = [ps.enter_context(nc.psum_tensor(f"PST{i}", [128, 195], F32)) for i in range(2)]
            PG = [ps.enter_context(nc.psum_tensor(f"PG{i}", [128, 384], F32)) for i in range(2)]
            PY = [ps.enter_context(nc.psum_tensor(f"PY{i}", [65, 256], F32)) for i in range(2)]
            PTR = ps.enter_context(nc.psum_tensor("PTR", [128, 768], BF16))
            PZB = ps.enter_context(nc.psum_tensor("PZB7", [64, 256], F32))

            p = Ph()
            p.dma("gpsimd", WO1[:, :], dram["wo1"], dq1)
            for bb in KFB:
                for b in bb:
                    p.add("vector", lambda v, b=b: v.memset(
                        b.rearrange("p (s f) -> p s f", s=2)[:, :, 0:1], 0.5))
                    p.add("vector", lambda v, b=b: v.memset(
                        b.rearrange("p (s f) -> p s f", s=2)[:, :, 273:384], 0.0))
            for b in QFTK:
                p.add("vector", lambda v, b=b: v.memset(
                    b.rearrange("p (s f) -> p s f", s=2)[:, :, 0:1], 1.0))
                p.add("vector", lambda v, b=b: v.memset(
                    b.rearrange("p (s f) -> p s f", s=2)[:, :, 273:384], 0.0))
            p.op("vector", "drain", inc=vs)

            units = [(hp + 8 * lane, ch) for hp in range(8) for ch in range(NCH)
                     for lane in (0, 1)]

            def build_qft(h, ch, buf, wait_tr):
                if wait_tr:
                    p.wait("vector", ts, wait_tr)
                for tsub in range(2):
                    tt = 2 * ch + tsub
                    src = QTK[:, tt * 256 + h * 16: tt * 256 + h * 16 + 16]
                    i0 = src.unsqueeze(1).broadcast_to([128, 16, 16])
                    i1 = src.unsqueeze(2).broadcast_to([128, 16, 16])
                    p.op("vector", "tensor_tensor",
                         QFTK[buf][:, tsub * 384 + 17: tsub * 384 + 273].rearrange(
                             "p (i j) -> p i j", i=16), i0, i1, AL.mult)
                    p.op("vector", "tensor_scalar_mul",
                         QFTK[buf][:, tsub * 384 + 1: tsub * 384 + 17], src, 0.5 / S1,
                         inc=(vs if tsub == 1 else None))
                return vs.n

            ptr_evac_vs = [0, 0]
            tr_ts = {}

            def issue_qft_T(buf, wait_vs, wait_ts):
                tr_end = 0
                for tsub in range(2):
                    p.wait("tensor", vs, wait_vs)
                    p.wait("tensor", vs, ptr_evac_vs[tsub])
                    for fcx in range(3):
                        p.op("tensor", "transpose", PTR[tsub][:, fcx * 128:(fcx + 1) * 128],
                             QFTK[buf][:, tsub * 384 + fcx * 128: tsub * 384 + fcx * 128 + 128],
                             IDN[:, :], inc=(ts if fcx == 2 else None))
                    p.wait("vector", ts)
                    p.wait("vector", ts, wait_ts)
                    p.op("vector", "tensor_copy",
                         QFM[buf].rearrange("p (c f) -> p c f", f=128)[:, tsub:6:2, :],
                         PTR[tsub].rearrange("p (c f) -> p c f", f=128), inc=vs)
                    ptr_evac_vs[tsub] = vs.n
                    tr_end = ts.n
                return tr_end

            g_ts = {}; araw_ss = {}; a_vs = {}; y_ts = {}
            yt_vs = {}; yv_vs = {}; state_ts = {}
            evac_ss = {}
            qfm_done = {}

            def prep_qf(tu):
                nh, nch = units[tu]
                if nch > 0:
                    bv = build_qft(nh, nch, tu % 2, qfm_done.get(tu - 2) or 0)
                    qfm_done[tu] = issue_qft_T(tu % 2, bv, y_ts.get(tu - 2, 0))
                else:
                    qfm_done[tu] = None

            qfm_done[0] = None
            if len(units) > 1:
                prep_qf(1)

            def U(i):
                return units[i]

            def v_build(i):
                h, ch = U(i)
                lane = h // 8
                kb = (i // 2) % 2
                p.wait("vector", ts, g_ts.get(i - 2, 0))
                p.op("vector", "tensor_single_scalar", RHQ[lane][:, :],
                     QF[:, lane * T + ch * 256: lane * T + ch * 256 + 256],
                     HMASK[:, (h % 8):(h % 8) + 1], AL.mult)
                p.wait("vector", ts, state_ts.get(i - 4, 0))
                src3 = KTK.rearrange("p (t r) -> p t r", r=256)[:, 2 * ch:2 * ch + 2,
                                                                h * 16:h * 16 + 16]
                i0 = src3.unsqueeze(2).broadcast_to([128, 2, 16, 16])
                i1 = src3.unsqueeze(3).broadcast_to([128, 2, 16, 16])
                kview = KFB[lane][kb].rearrange("p (s r) -> p s r", s=2)
                p.op("vector", "tensor_tensor",
                     kview[:, :, 17:273].rearrange("p s (i j) -> p s i j", i=16),
                     i0, i1, AL.mult)
                p.op("vector", "tensor_scalar_mul", kview[:, :, 1:17], src3,
                     0.25 * math.sqrt(2.0) / S1)
                p.op("vector", "drain", inc=vs)
                build_vs[i] = vs.n

            def v_qft(i):
                h, ch = U(i)
                if ch == 0:
                    return
                buf = i % 2
                p.wait("vector", ts, tr_ts.get(i - 2, 0))
                src3 = QTK.rearrange("p (t r) -> p t r", r=256)[:, 2 * ch:2 * ch + 2,
                                                                h * 16:h * 16 + 16]
                i0 = src3.unsqueeze(2).broadcast_to([128, 2, 16, 16])
                i1 = src3.unsqueeze(3).broadcast_to([128, 2, 16, 16])
                qview = QFTK[buf].rearrange("p (s r) -> p s r", s=2)
                p.op("vector", "tensor_tensor",
                     qview[:, :, 17:273].rearrange("p s (i j) -> p s i j", i=16),
                     i0, i1, AL.mult)
                p.op("vector", "tensor_scalar_mul", qview[:, :, 1:17], src3, 0.5 / S1,
                     inc=vs)
                qft_vs[i] = vs.n

            def t_tr(i):
                h, ch = U(i)
                if ch == 0:
                    tr_ts[i] = ts.n
                    return
                p.wait("tensor", vs, qft_vs[i])
                p.wait("tensor", ss, ptr_evac_vs.get(i - 2, 0))
                for tsub in range(2):
                    for fcx in range(3):
                        p.op("tensor", "transpose",
                             PTR[:, tsub * 384 + fcx * 128: tsub * 384 + fcx * 128 + 128],
                             QFTK[i % 2][:, tsub * 384 + fcx * 128: tsub * 384 + fcx * 128 + 128],
                             IDN[:, :],
                             inc=(ts if (tsub, fcx) == (1, 2) else None))
                tr_ts[i] = ts.n

            def v_qfm_evac(i):
                h, ch = U(i)
                if ch == 0:
                    return
                p.wait("scalar", ts, tr_ts[i])
                p.wait("scalar", ts, y_ts.get(i - 3, 0))
                src = PTR.rearrange("p (t c f) -> p t c f", t=2, c=3)
                dst = QFM[i % 3].rearrange("p (c t f) -> p t c f", c=3, t=2)
                p.op("scalar", "activation", dst, src, AF.Copy, inc=ss)
                ptr_evac_vs[i] = ss.n

            def s_evac(i):
                h, ch = U(i)
                lane = h // 8
                if ch == 0:
                    return
                p.wait("scalar", ts, state_ts.get(i - 2, 0))
                p.wait("scalar", ts, y_ts.get(i - 2, 0))
                p.op("scalar", "activation", STSB[lane][:, :], PST[lane][:, :],
                     AF.Copy, inc=ss)
                evac_ss[i] = ss.n

            def t_g_state(i):
                h, ch = U(i)
                lane = h // 8
                kb = (i // 2) % 2
                p.wait("tensor", vs, build_vs[i])
                if ch > 0:
                    p.wait("tensor", ss, evac_ss[i])
                p.wait("tensor", ss, araw_ss.get(i - 2, 0))
                p.op("tensor", "matmul", PG[lane][:, 0:256],
                     KF[:, lane * T + (2 * ch) * 128: lane * T + (2 * ch) * 128 + 128],
                     RHQ[lane][:, :], start=True, stop=True)
                p.op("tensor", "matmul", PG[lane][:, 256:384],
                     KF[:, lane * T + (2 * ch + 1) * 128: lane * T + (2 * ch + 1) * 128 + 128],
                     RHQ[lane][:, 128:256], start=True, stop=True, inc=ts)
                g_ts[i] = ts.n
                for tsub in range(2):
                    first = (ch == 0 and tsub == 0)
                    for fcx in range(3):
                        p.op("tensor", "matmul",
                             PST[lane][:, fcx * 65: fcx * 65 + 65],
                             KFB[lane][kb][:, tsub * 384 + fcx * 128: tsub * 384 + fcx * 128 + 128],
                             V1[:, (2 * ch + tsub) * 1040 + h * 65: (2 * ch + tsub) * 1040 + h * 65 + 65],
                             start=(first and fcx == 0), stop=(tsub == 1 and fcx == 2),
                             skip_group_check=True,
                             inc=(ts if (tsub == 1 and fcx == 2) else None))
                state_ts[i] = ts.n

            def s_square(i):
                h, ch = U(i)
                lane = h // 8
                p.wait("scalar", ts, g_ts[i])
                p.wait("scalar", vs, a_vs.get(i - 2, 0))
                p.op("scalar", "activation", ARAW[lane][:, 0:384], PG[lane][:, 0:384],
                     AF.Square, scale=0.125, bias=CB[:, 2:3], inc=ss)
                araw_ss[i] = ss.n

            def v_poly(i):
                h, ch = U(i)
                lane = h // 8
                p.wait("vector", ss, araw_ss[i])
                p.wait("vector", ts, y_ts.get(i - 2, 0))
                p.op("vector", "scalar_tensor_tensor", ASB[lane][:, 0:128],
                     ARAW[lane][:, 0:128], 0.25, TRIL[:, :], AL.add, AL.mult)
                p.op("vector", "tensor_scalar_add", ASB[lane][:, 128:256],
                     ARAW[lane][:, 128:256], 0.25)
                p.op("vector", "scalar_tensor_tensor", ASB[lane][:, 256:384],
                     ARAW[lane][:, 256:384], 0.25, TRIL[:, :], AL.add, AL.mult, inc=vs)
                a_vs[i] = vs.n

            def t_y(i):
                h, ch = U(i)
                lane = h // 8
                p.wait("tensor", vs, a_vs[i])
                p.wait("tensor", vs, yv_vs.get(i - 2, 0))
                mms = [
                    (PY[lane][:, 0:256],
                     V1[:, (2 * ch) * 1040 + h * 65: (2 * ch) * 1040 + h * 65 + 65],
                     ASB[lane][:, 0:256]),
                    (PY[lane][:, 128:256],
                     V1[:, (2 * ch + 1) * 1040 + h * 65: (2 * ch + 1) * 1040 + h * 65 + 65],
                     ASB[lane][:, 256:384]),
                ]
                if ch > 0:
                    mms += [
                        (PY[lane][:, 0:256], STSB[lane][:, 0:65], QFM[i % 3][:, 0:256]),
                        (PY[lane][:, 0:256], STSB[lane][:, 65:130], QFM[i % 3][:, 256:512]),
                        (PY[lane][:, 0:256], STSB[lane][0:17, 130:195], QFM[i % 3][0:17, 512:768]),
                    ]
                for mi, (o, l, r) in enumerate(mms):
                    p.op("tensor", "matmul", o, l, r, start=(mi == 0),
                         stop=(mi == len(mms) - 1),
                         inc=(ts if mi == len(mms) - 1 else None))
                y_ts[i] = ts.n

            def s_z(i):
                h, ch = U(i)
                lane = h // 8
                p.wait("scalar", ts, y_ts[i])
                p.wait("scalar", ts, zb_ts.get(i - 2, 0))
                p.op("scalar", "activation", ZSC1[lane][64:65, :], PY[lane][64:65, :], AF.Ln)
                p.op("scalar", "drain")
                p.op("scalar", "activation", Z1[lane][64:65, :], ZSC1[lane][64:65, :],
                     AF.Exp, scale=-1.0, inc=ss)
                z_ss[i] = ss.n

            def t_zb(i):
                h, ch = U(i)
                lane = h // 8
                p.wait("tensor", ss, z_ss[i])
                p.wait("tensor", vs, yv_vs.get(i - 1, 0))
                p.op("tensor", "matmul", PZB[:, :], ONES_B[64:65, 0:64],
                     Z1[lane][64:65, :], start=True, stop=True, inc=ts)
                zb_ts[i] = ts.n

            def v_yout(i):
                h, ch = U(i)
                lane = h // 8
                p.wait("scalar", ts, y_ts[i])
                p.wait("scalar", vs, yv_vs.get(i - 2, 0))
                p.op("scalar", "activation", YT1[lane][:, :], PY[lane][0:64, :], AF.Copy,
                     inc=ss)
                yt_s = ss.n
                p.wait("vector", ss, yt_s)
                p.wait("vector", ts, zb_ts[i])
                p.op("vector", "tensor_mul",
                     Y[64 * (h % 2): 64 * (h % 2) + 64,
                       (h // 2) * T + ch * 256: (h // 2) * T + ch * 256 + 256],
                     YT1[lane][:, :], PZB[0:64, :], inc=vs)
                yv_vs[i] = vs.n

            g_ts = {}; araw_ss = {}; a_vs = {}; y_ts = {}
            yv_vs = {}; state_ts = {}; evac_ss = {}
            build_vs = {}; qft_vs = {}; tr_ts = {}; ptr_evac_vs = {}
            zb_ts = {}; z_ss = {}
            NU = len(units)

            v_build(0)
            for j in (1, 2):
                if j < NU:
                    v_qft(j)

            for i in range(NU):
                t_g_state(i)
                if i >= 1:
                    t_zb(i - 1)
                if i + 2 < NU:
                    t_tr(i + 2)
                if i + 1 < NU:
                    s_evac(i + 1)
                s_square(i)
                v_poly(i)
                if i + 2 < NU:
                    v_qfm_evac(i + 2)
                if i + 1 < NU:
                    v_build(i + 1)
                if i + 3 < NU:
                    v_qft(i + 3)
                t_y(i)
                s_z(i)
                if i >= 1:
                    v_yout(i - 1)
            t_zb(NU - 1)
            v_yout(NU - 1)
            p.emit(nc)
        tap("y1", Y)

    if act("p8"):
        with ExitStack() as ps:
            PWO = ps.enter_context(nc.psum_tensor("PWO1", [128, 1024], F32))
            T1 = ps.enter_context(nc.sbuf_tensor("T1b", [128, 1024], F32))
            p = Ph()
            p.wait("tensor", dq1)
            wo_ev0 = ss.n
            add_v0 = vs.n
            for g in range(NFC * NT):
                fc, tc = divmod(g, NT)
                if g >= 2:
                    p.wait("tensor", ss, wo_ev0 + g - 1)
                pw = PWO[:, (g % 2) * 512:(g % 2) * 512 + 512]
                for c in range(NFC):
                    p.op("tensor", "matmul", pw,
                         WO1[:, c * 1024 + fc * 128: c * 1024 + fc * 128 + 128],
                         Y[:, c * T + tc * 512: c * T + tc * 512 + 512],
                         start=(c == 0), stop=(c == NFC - 1),
                         inc=(ts if c == NFC - 1 else None))
                p.wait("scalar", ts)
                if g >= 2:
                    p.wait("scalar", vs, add_v0 + g - 1)
                p.op("scalar", "activation", T1[:, (g % 2) * 512:(g % 2) * 512 + 512], pw,
                     AF.Copy, inc=ss)
                p.wait("vector", ss)
                p.op("vector", "tensor_add",
                     X[:, fc * T + tc * 512: fc * T + tc * 512 + 512],
                     X[:, fc * T + tc * 512: fc * T + tc * 512 + 512],
                     T1[:, (g % 2) * 512:(g % 2) * 512 + 512], inc=vs)
            p.emit(nc)
    if act("p7"):
        es1y.close()
    if act("p5"):
        es1.close()
    tap("x3", X)

    if act("p9"):
        es3 = ExitStack()
        XLN = sbuf("XLN3", [128, NFC * T], BF16, es3)
        ln_phase(XLN)

    if act("p10"):
        with ExitStack() as ps:
            HH = ps.enter_context(nc.sbuf_tensor("HH", [128, (HID1 // 128) * TCH], BF16))
            WGS = [ps.enter_context(nc.sbuf_tensor(f"WGS{i}", [128, NFC * 512], BF16)) for i in range(2)]
            WUS = [ps.enter_context(nc.sbuf_tensor(f"WUS{i}", [128, NFC * 512], BF16)) for i in range(2)]
            WDS = [ps.enter_context(nc.sbuf_tensor(f"WDS{i}", [128, (HID1 // 128) * 128], BF16)) for i in range(2)]
            SGT = ps.enter_context(nc.sbuf_tensor("SGT", [128, 512], BF16))
            TT1 = ps.enter_context(nc.sbuf_tensor("TT1", [128, 512], BF16))
            TF2 = ps.enter_context(nc.sbuf_tensor("TF2", [128, 1024], F32))
            PGU = ps.enter_context(nc.psum_tensor("PGU", [128, 2048], F32))
            PDN = ps.enter_context(nc.psum_tensor("PDN", [128, 1024], F32))
            NFB1 = HID1 // 512
            p = Ph()
            wg_done, wd_done = [], []
            hh_hist = {}
            for i in range(2):
                p.dma("gpsimd", WGS[i].rearrange("p (c f) -> p c f", f=512), dram["wg1"][i].rearrange("c p f -> p c f"), dgu[i])
                p.dma("gpsimd", WUS[i].rearrange("p (c f) -> p c f", f=512), dram["wu1"][i].rearrange("c p f -> p c f"), dgu[i])
                wg_done.append((dgu[i], dgu[i].n))
                p.dma("gpsimd", WDS[i].rearrange("p (k f) -> p k f", f=128), dram["wd1"][i].rearrange("k p f -> p k f"), dwd[i])
                wd_done.append((dwd[i], dwd[i].n))
            for tch in range(NTCH):
                for fb in range(NFB1):
                    gi = tch * NFB1 + fb
                    p.wait("tensor", *wg_done[gi])
                    for fi in range(4):
                        for col in range(NCOL):
                            g = gi * 4 * NCOL + fi * NCOL + col
                            if g > 1:
                                p.wait("tensor", vs, hh_hist.get(g - 2, 0))
                            pg_ = PGU[:, (g % 2) * 512:(g % 2) * 512 + 512]
                            pu_ = PGU[:, 1024 + (g % 2) * 512: 1024 + (g % 2) * 512 + 512]
                            for c in range(NFC):
                                p.op("tensor", "matmul", pg_,
                                     WGS[fb % 2][:, c * 512 + fi * 128: c * 512 + fi * 128 + 128],
                                     XLN[:, c * T + tch * TCH + col * 512: c * T + tch * TCH + col * 512 + 512],
                                     start=(c == 0), stop=(c == NFC - 1),
                                     inc=(ts if c == NFC - 1 else None))
                            g_ts_ = ts.n
                            for c in range(NFC):
                                p.op("tensor", "matmul", pu_,
                                     WUS[fb % 2][:, c * 512 + fi * 128: c * 512 + fi * 128 + 128],
                                     XLN[:, c * T + tch * TCH + col * 512: c * T + tch * TCH + col * 512 + 512],
                                     start=(c == 0), stop=(c == NFC - 1),
                                     inc=(ts if c == NFC - 1 else None))
                            u_ts_ = ts.n
                            p.wait("scalar", ts, g_ts_)
                            p.wait("scalar", vs, hh_vs if g > 0 else 0)
                            p.op("scalar", "activation", SGT[:, :], pg_, AF.Sigmoid,
                                 scale=Q127, inc=ss)
                            p.wait("vector", ss)
                            p.op("vector", "tensor_mul", TT1[:, :], SGT[:, :], pg_)
                            p.wait("vector", ts, u_ts_)
                            p.op("vector", "drain")
                            fidx = fb * 4 + fi
                            p.op("vector", "tensor_mul",
                                 HH[:, fidx * TCH + col * 512: fidx * TCH + col * 512 + 512],
                                 TT1[:, :], pu_, inc=vs)
                            hh_vs = vs.n
                            hh_hist[g] = vs.n
                    gu_mm = ts.n
                    nxt = gi + 2
                    if nxt < NTCH * NFB1:
                        p.wait("gpsimd", ts, gu_mm)
                        p.dma("gpsimd", WGS[nxt % 2].rearrange("p (c f) -> p c f", f=512),
                              dram["wg1"][nxt % NFB1].rearrange("c p f -> p c f"), dgu[nxt % 2])
                        p.dma("gpsimd", WUS[nxt % 2].rearrange("p (c f) -> p c f", f=512),
                              dram["wu1"][nxt % NFB1].rearrange("c p f -> p c f"), dgu[nxt % 2])
                        wg_done.append((dgu[nxt % 2], dgu[nxt % 2].n))
                h1_vs = vs.n
                p.wait("tensor", ss)
                p.wait("scalar", vs)
                dn_ev0 = ss.n
                add_v0 = vs.n
                for dcc in range(NFC):
                    gi = tch * NFC + dcc
                    p.wait("tensor", *wd_done[gi])
                    p.wait("tensor", vs, h1_vs)
                    for col in range(NCOL):
                        g = dcc * NCOL + col
                        if g >= 2:
                            p.wait("tensor", ss, dn_ev0 + g - 1)
                        pq = PDN[:, (g % 2) * 512:(g % 2) * 512 + 512]
                        for k in range(HID1 // 128):
                            p.op("tensor", "matmul", pq,
                                 WDS[dcc % 2][:, k * 128:(k + 1) * 128],
                                 HH[:, k * TCH + col * 512: k * TCH + col * 512 + 512],
                                 start=(k == 0), stop=(k == HID1 // 128 - 1),
                                 inc=(ts if k == HID1 // 128 - 1 else None))
                        p.wait("scalar", ts)
                        if g >= 2:
                            p.wait("scalar", vs, add_v0 + g - 1)
                        p.op("scalar", "activation", TF2[:, (g % 2) * 512:(g % 2) * 512 + 512],
                             pq, AF.Identity, scale=Q127 ** 3, inc=ss)
                        p.wait("vector", ss)
                        p.op("vector", "tensor_add",
                             X[:, dcc * T + tch * TCH + col * 512: dcc * T + tch * TCH + col * 512 + 512],
                             X[:, dcc * T + tch * TCH + col * 512: dcc * T + tch * TCH + col * 512 + 512],
                             TF2[:, (g % 2) * 512:(g % 2) * 512 + 512], inc=vs)
                    dn_mm = ts.n
                    nxt = gi + 2
                    if nxt < NTCH * NFC:
                        p.wait("gpsimd", ts, dn_mm)
                        p.dma("gpsimd", WDS[nxt % 2].rearrange("p (k f) -> p k f", f=128),
                              dram["wd1"][nxt % NFC].rearrange("k p f -> p k f"), dwd[nxt % 2])
                        wd_done.append((dwd[nxt % 2], dwd[nxt % 2].n))
                    if tch == NTCH - 1 and NTCH > 1:
                        so = [dout, dout2][dcc % 2]
                        p.wait("sync", so)
                        p.wait("sync", vs)
                        off = dcc * T + tch * TCH
                        p.dma("sync", dram["out"][:, off:off + TCH],
                              X[:, off:off + TCH], so)
                if tch < NTCH - 1:
                    p.wait("sync", vs)
                    p.dma("sync",
                          dram["out"].rearrange("p (c t) -> p c t", t=T)[:, :, tch * TCH:(tch + 1) * TCH],
                          X.rearrange("p (c t) -> p c t", t=T)[:, :, tch * TCH:(tch + 1) * TCH],
                          dout)
            p.emit(nc)

    p = Ph()
    if act("p10") and NTCH > 1:
        p.wait("sync", dout)
        p.wait("sync", dout2)
    else:
        p.dma("sync", dram["out"], X[:, :], dout)
        p.wait("sync", dout)
    p.emit(nc)

    if act("p9"):
        es3.close()
    st.close()
    nc.finalize()
    return nc



def kernel(**inputs):
    T = 2048
    x = np.asarray(inputs["L0_attn_x"], np.float32)
    H = host_prepare(inputs)
    nc = build_program(T=T)
    in_maps = []
    for c in range(8):
        m = dict(H)
        m["x"] = host_x_core(x[c], T)
        in_maps.append(m)
    res = run_bass_kernel_spmd(nc, in_maps, core_ids=list(range(8)))
    out = np.stack([host_out_core(np.asarray(r["out"]), T) for r in res.results])
    dt = np.float64 if out.dtype == np.float64 else np.float32
    return np.ascontiguousarray(out, dtype=dt)
